# version 03
# baseline (speedup 1.0000x reference)
N_CORES = 8
BLOCK = 64
N_BLOCKS = 128
SEQ = 8192
D = 4096
NKC = D // 128
OWN_TOK = 1024
LOCAL_BLOCKS = 16
VERT = 8
NH = 32
NKV = 8
HD = 128
I_FFN = 11008
NIC = I_FFN // 128
IC_PER_CORE = 1376
CS = 8
CB = 65536


def own_blocks(c):
    return [c + 8 * j for j in range(16)]


def own_token_ids(c):
    import numpy as np
    ids = []
    for b in own_blocks(c):
        ids.extend(range(b * BLOCK, (b + 1) * BLOCK))
    return np.array(ids)


def l0_allowed(qb, b):
    return b <= qb and (qb - b < LOCAL_BLOCKS or (b + 1) % VERT == 0)


def l0_plan(qb):
    lo = max(0, qb - (LOCAL_BLOCKS - 1))
    ops = []
    t0, t1 = lo // 2, qb // 2
    for t in range(t0, t1 + 1):
        bl, bh = 2 * t, 2 * t + 1
        ok_l = lo <= bl <= qb
        ok_h = lo <= bh <= qb
        if ok_l and ok_h:
            ml = "causal" if bl == qb else None
            mh = "causal" if bh == qb else None
            ops.append(("tile", t, ml, mh))
        elif ok_l:
            ops.append(("block", bl, "causal" if bl == qb else None))
        elif ok_h:
            ops.append(("block", bh, "causal" if bh == qb else None))
    b = 7
    while b < lo:
        ops.append(("block", b, None))
        b += 8
    return ops


def l1_plan(c, qh_chunk_tokens=512):
    plans = []
    blocks = own_blocks(c)
    for t in range(N_BLOCKS // 2):
        bl, bh = 2 * t, 2 * t + 1
        j0 = None
        for j, b in enumerate(blocks):
            if b >= bl:
                j0 = j
                break
        if j0 is None:
            continue
        masks = []
        b0 = blocks[j0]
        if b0 == bl:
            masks.append((0, j0, "causal"))
            masks.append((1, j0, "zero"))
        elif b0 == bh:
            masks.append((1, j0, "causal"))
        elif b0 < bh:
            raise AssertionError
        plans.append((t, j0, masks))
    return plans


def causal_mask64():
    import numpy as np
    i = np.arange(BLOCK)
    return (i[:, None] <= i[None, :]).astype(np.float32)


def ffn_shard_rows(c):
    return c * IC_PER_CORE, (c + 1) * IC_PER_CORE


def rope_tables():
    import numpy as np
    inv = 1.0 / (1000000.0 ** (np.arange(0, HD, 2, dtype=np.float64) / HD))
    t = np.arange(SEQ, dtype=np.float64)
    freqs = np.outer(t, inv)
    emb = np.concatenate([freqs, freqs], axis=1)
    return np.cos(emb).astype(np.float32), np.sin(emb).astype(np.float32)



def l0_strided_slots(j):
    return max(0, j - 1)


def l0_strided_pairs(j):
    return (l0_strided_slots(j) + 1) // 2


def build_l0_masks(c):
    import numpy as np
    cm = causal_mask64()
    Mwin = np.zeros((2, 128, 512), np.float32)
    for j in range(2):
        qb = c + 8 * j
        for w in range(8):
            for half in range(2):
                b = 2 * w + half
                rows = slice(half * 64, (half + 1) * 64)
                cols = slice(w * 64, (w + 1) * 64)
                if b < qb:
                    Mwin[j][rows, cols] = 1.0
                elif b == qb:
                    Mwin[j][rows, cols] = cm
    Mstr = np.zeros((16, 128, 7 * 64), np.float32)
    for j in range(16):
        qb = c + 8 * j
        for sp in range(7):
            for half in range(2):
                m = 2 * sp + half
                rows = slice(half * 64, (half + 1) * 64)
                cols = slice(sp * 64, (sp + 1) * 64)
                if 8 * m + 7 < qb - 15:
                    Mstr[j][rows, cols] = 1.0
    return Mwin, Mstr



def l1_j0u(t):
    return max(0, -(-(2 * t - 7) // 8))


def build_l1_masks(c):
    import numpy as np
    cm = causal_mask64()
    M = np.zeros((64, 128, 128), np.float32)
    for t in range(64):
        j0 = l1_j0u(t)
        for dj in range(2):
            j = j0 + dj
            if j > 15:
                continue
            qb = c + 8 * j
            cols = slice(dj * 64, (dj + 1) * 64)
            for half in range(2):
                b = 2 * t + half
                rows = slice(half * 64, (half + 1) * 64)
                if b < qb:
                    M[t][rows, cols] = 1.0
                elif b == qb:
                    M[t][rows, cols] = cm
    return M


"""Bass emitters for the mk_seq_preln_d2 kernel.

Phases append (engine, closure) ops to per-engine lists; Block sections replay
them. Semaphore counts tracked at append time. All launches SPMD-uniform.
"""
import numpy as np
from collections import defaultdict
from contextlib import ExitStack

import concourse.bass as bass
import concourse.mybir as mybir



F32 = mybir.dt.float32
BF16 = mybir.dt.bfloat16
I32 = mybir.dt.int32
U32 = mybir.dt.uint32
AF = mybir.ActivationFunctionType
ALU = mybir.AluOpType
IOA = bass.IndirectOffsetOnAxis
ds = bass.ds

ENGINES = ("tensor", "vector", "scalar", "gpsimd", "sync")


class Prog:
    def __init__(self, race=True):
        self.nc = bass.Bass("TRN2", target_bir_lowering=False, debug=False,
                            detect_race_conditions=race)
        self.stack = ExitStack()
        self.ops = {e: [] for e in ENGINES}
        self.sem = {}
        self.cnt = defaultdict(int)
        self.sb = {}
        self.dram = {}
        self.guards = {}

    def sem_new(self, *names):
        for n in names:
            if n not in self.sem:
                self.sem[n] = self.stack.enter_context(self.nc.semaphore(n))

    def sbuf(self, name, shape, dtype):
        t = self.stack.enter_context(self.nc.sbuf_tensor(name, shape, dtype))
        self.sb[name] = t
        return t

    def psum(self, name, shape, dtype=F32):
        return self.stack.enter_context(self.nc.psum_tensor(name, shape, dtype))

    def din(self, name, shape, dtype):
        ap = self.nc.dram_tensor(name, shape, dtype, kind="ExternalInput").ap()
        self.dram[name] = ap
        return ap

    def dout(self, name, shape, dtype):
        ap = self.nc.dram_tensor(name, shape, dtype, kind="ExternalOutput").ap()
        self.dram[name] = ap
        return ap

    def emit(self, eng, fn):
        self.ops[eng].append(fn)

    def inc(self, sem_name, by):
        self.cnt[sem_name] += by
        return self.cnt[sem_name]

    def dma(self, q, out, in_, sem):
        n = self.inc(sem, 16)
        s = self.sem[sem]
        self.emit(q, lambda e, out=out, in_=in_: e.dma_start(out=out, in_=in_).then_inc(s, 16))
        return n

    def wait(self, eng, sem, n):
        if n is None or n <= 0:
            return
        s = self.sem[sem]
        self.emit(eng, lambda e, n=n: e.wait_ge(s, n))

    def op(self, eng, f, sem=None):
        if sem is None:
            self.emit(eng, f)
            return None
        n = self.inc(sem, 1)
        s = self.sem[sem]
        self.emit(eng, lambda e, f=f: f(e).then_inc(s, 1))
        return n

    def guard_set(self, key, sem, n):
        self.guards[key] = (sem, n)

    def guard_wait(self, eng, key):
        if key in self.guards:
            s, n = self.guards[key]
            self.wait(eng, s, n)

    def build(self):
        with self.nc.Block() as block:
            for e in ENGINES:
                ops = self.ops[e]
                if not ops:
                    continue

                def body(eng, ops=ops):
                    for f in ops:
                        f(eng)
                getattr(block, e)(body)
        self.stack.close()
        return self.nc


def alloc_common(P):
    P.sbuf("ones_bf", [128, 1], BF16)
    P.sbuf("ones_f", [128, 1], F32)
    P.sbuf("ones1_f", [1, 128], F32)


def load_common(P, sem="c0"):
    P.sem_new(sem)
    P.dma("sync", P.sb["ones_bf"][:, :], P.din("c_ones_bf", [128, 1], BF16), sem)
    P.dma("sync", P.sb["ones_f"][:, :], P.din("c_ones_f", [128, 1], F32), sem)
    n = P.dma("sync", P.sb["ones1_f"][:, :], P.din("c_ones1_f", [1, 128], F32), sem)
    return n


def common_host():
    import ml_dtypes
    return {
        "c_ones_bf": np.ones((128, 1), np.float32).astype(ml_dtypes.bfloat16),
        "c_ones_f": np.ones((128, 1), np.float32),
        "c_ones1_f": np.ones((1, 128), np.float32),
    }


def alloc_ln_scratch(P, TT):
    P.sbuf("ln_xq", [128, NKC, TT], BF16)
    P.sbuf("ln_st", [1, 8, TT], F32)
    P.sbuf("ln_tmp", [128, 2, TT], F32)


def phase_ln(P, x_ap, TT, xn_view, pfx, ps1, ps2, psb, psm):
    P.sem_new(pfx + "a", pfx + "v", pfx + "t")
    xq, st, tmp = P.sb["ln_xq"], P.sb["ln_st"], P.sb["ln_tmp"]
    ones_bf, ones_f, ones1_f = P.sb["ones_bf"], P.sb["ones_f"], P.sb["ones1_f"]

    P.guard_wait("scalar", "ln_xq")
    na = P.op("scalar", lambda e: e.activation(xq[:, :, :TT], x_ap, AF.Square), pfx + "a")
    P.wait("tensor", pfx + "a", na)
    nt = None
    for kc in range(NKC):
        s0, s1_ = (kc == 0), (kc == NKC - 1)
        P.op("tensor", lambda e, kc=kc, s0=s0, s1_=s1_: e.matmul(
            ps1, ones_f[:, :], x_ap[:, kc, :], start=s0, stop=s1_))
        nt = P.op("tensor", lambda e, kc=kc, s0=s0, s1_=s1_: e.matmul(
            ps2, ones_bf[:, :], xq[:, kc, :TT], start=s0, stop=s1_), pfx + "t")
    P.guard_set("ln_xq", pfx + "t", nt)
    P.wait("vector", pfx + "t", nt)
    rD = 1.0 / D
    P.op("vector", lambda e: e.tensor_scalar(st[:, 0, :TT], ps1, rD, None, ALU.mult))
    P.op("vector", lambda e: e.tensor_scalar(st[:, 1, :TT], ps2, rD, None, ALU.mult))
    P.op("vector", lambda e: e.tensor_tensor(st[:, 2, :TT], st[:, 0, :TT], st[:, 0, :TT], ALU.mult))
    P.op("vector", lambda e: e.tensor_tensor(st[:, 3, :TT], st[:, 1, :TT], st[:, 2, :TT], ALU.subtract))
    nv = P.op("vector", lambda e: e.tensor_scalar(st[:, 3, :TT], st[:, 3, :TT], 1e-5, None, ALU.add), pfx + "v")
    P.wait("scalar", pfx + "v", nv)
    na = P.op("scalar", lambda e: e.activation(st[:, 4, :TT], st[:, 3, :TT], AF.Sqrt), pfx + "a")
    P.wait("vector", pfx + "a", na)
    P.op("vector", lambda e: e.reciprocal(st[:, 5, :TT], st[:, 4, :TT]))
    nv = P.op("vector", lambda e: e.tensor_tensor(st[:, 6, :TT], st[:, 0, :TT], st[:, 5, :TT], ALU.mult), pfx + "v")
    P.wait("tensor", pfx + "v", nv)
    P.op("tensor", lambda e: e.matmul(psb, ones1_f[:, :], st[:, 5, :TT], start=True, stop=True))
    nt = P.op("tensor", lambda e: e.matmul(psm, ones1_f[:, :], st[:, 6, :TT], start=True, stop=True), pfx + "t")
    P.wait("vector", pfx + "t", nt)
    nv = None
    for kc in range(NKC):
        b = kc % 2
        P.op("vector", lambda e, kc=kc, b=b: e.tensor_tensor(tmp[:, b, :TT], x_ap[:, kc, :], psb, ALU.mult))
        nv = P.op("vector", lambda e, kc=kc, b=b: e.tensor_tensor(
            xn_view[:, kc, :], tmp[:, b, :TT], psm, ALU.subtract), pfx + "v")
    return nv


def phase_wstripe_T(P, w_dram, row0, wf, wb, wt_view, ident_bf, pbT, pfx, ncin=NKC):
    P.sem_new(pfx + "d", pfx + "a", pfx + "t", pfx + "v")
    half = ncin * 128 // 2
    for hh in range(2):
        P.guard_wait("sync", f"{pfx}_wf{hh}")
        nd = P.dma("sync", wf[:, hh * half:(hh + 1) * half],
                   w_dram[row0:row0 + 128, hh * half:(hh + 1) * half], pfx + "d")
        P.wait("scalar", pfx + "d", nd)
        P.guard_wait("scalar", pfx + "_wb")
        na = P.op("scalar", lambda e, hh=hh: e.activation(
            wb[:, hh * half:(hh + 1) * half], wf[:, hh * half:(hh + 1) * half],
            AF.Copy, bias=0.0), pfx + "a")
        P.guard_set(f"{pfx}_wf{hh}", pfx + "a", na)
    P.wait("tensor", pfx + "a", na)
    nv = None
    for grp in range(ncin // 8):
        bank = pbT[grp % 2]
        P.guard_wait("tensor", f"{pfx}_pbT{grp % 2}")
        nt = None
        for i in range(8):
            kc = grp * 8 + i
            nt = P.op("tensor", lambda e, kc=kc, i=i, bank=bank: e.transpose(
                bank[:, i * 128:(i + 1) * 128], wb[:, kc * 128:(kc + 1) * 128],
                ident_bf[:, :]), pfx + "t")
        P.wait("vector", pfx + "t", nt)
        P.guard_wait("vector", pfx + "_wtv")
        nv = P.op("vector", lambda e, grp=grp, bank=bank: e.tensor_copy(
            wt_view[:, grp * 8:(grp + 1) * 8, :].rearrange("p a b -> p (a b)"),
            bank[:, :]), pfx + "v")
        P.guard_set(f"{pfx}_pbT{grp % 2}", pfx + "v", nv)
    P.guard_set(pfx + "_wb", pfx + "t", P.cnt[pfx + "t"])
    return nv


"""Launch builders for the unfused mk kernel."""
import numpy as np
import ml_dtypes

import concourse.mybir as mybir

AXX = mybir.AxisListType.X




BF = ml_dtypes.bfloat16


def fm(x_tok_major):
    Tn, D = x_tok_major.shape
    return np.ascontiguousarray(
        x_tok_major.reshape(Tn, D // 128, 128).transpose(2, 1, 0))


def un_fm(x_fm):
    return np.ascontiguousarray(x_fm.transpose(2, 1, 0).reshape(x_fm.shape[2], -1))



def build_l2_qkv0():
    P = Prog()
    x = P.din("x_fm", [128, NKC, 1024], F32)
    w = P.din("w_qkv", [6144, 4096], F32)
    bq = P.din("bqkv", [128, 48], F32)
    cos = P.din("cos_own", [128, 1024], F32)
    sin = P.din("sin_own", [128, 1024], F32)
    rotT = P.din("c_rot_T", [128, 128], F32)
    identb = P.din("c_ident_bf", [128, 128], BF16)
    q_out = P.dout("q_fm", [32, 128, 1024], BF16)
    k_out = P.dout("k_fm", [8, 128, 1024], BF16)
    v_out = P.dout("v_tm", [8, 1024, 128], BF16)

    alloc_common(P)
    alloc_ln_scratch(P, 256)
    x_t = P.sbuf("x_t", [128, NKC, 256], F32)
    xn = P.sbuf("xn", [128, NKC, 1024], BF16)
    bq_sb = P.sbuf("bq_sb", [128, 48], F32)
    cos_sb = P.sbuf("cos_sb", [128, 1024], F32)
    sin_sb = P.sbuf("sin_sb", [128, 1024], F32)
    rot_sb = P.sbuf("rot_sb", [128, 128], F32)
    idb_sb = P.sbuf("idb_sb", [128, 128], BF16)
    wf = P.sbuf("wf", [128, 4096], F32)
    wb = P.sbuf("wb", [128, 4096], BF16)
    af = wf[0:64, :]
    ab = wb[0:64, :]
    rf = wf[0:16, :]
    rb_ = wb[0:16, :]
    wt = P.sbuf("wt", [128, 2, NKC, 128], BF16)
    praw = P.sbuf("praw", [128, 2, 512], F32)
    rope1 = P.sbuf("rope1", [128, 2, 512], F32)
    obuf = P.sbuf("obuf", [128, 2, 512], BF16)
    vtm = P.sbuf("vtm", [128, 2, 4, 128], BF16)

    pb = [P.psum(f"pb{i}", [128, 512], F32) for i in range(4)]
    pbT = [P.psum(f"pbT{i}", [128, 1024], BF16) for i in range(2)]
    pbR = P.psum("pbR", [128, 512], F32)
    pbV = P.psum("pbV", [128, 512], BF16)

    P.sem_new("c0", "ld", "wd", "wa", "wt", "wv", "pt", "pv", "od", "xd")
    n0 = load_common(P)
    P.dma("sync", bq_sb[:, :], bq, "ld")
    P.dma("sync", cos_sb[:, :], cos, "ld")
    P.dma("sync", sin_sb[:, :], sin, "ld")
    P.dma("sync", rot_sb[:, :], rotT, "ld")
    nld = P.dma("sync", idb_sb[:, :], identb, "ld")
    for e in ("scalar", "vector", "tensor"):
        P.wait(e, "ld", nld)
        P.wait(e, "c0", n0)

    for st_ in range(4):
        P.guard_wait("sync", "x_t")
        nxd = P.dma("sync", x_t[:, :, :], x[:, :, st_ * 256:(st_ + 1) * 256], "xd")
        P.wait("scalar", "xd", nxd)
        P.wait("tensor", "xd", nxd)
        P.wait("vector", "xd", nxd)
        nv = phase_ln(P, x_t[:, :, :], 256, xn[:, :, st_ * 256:(st_ + 1) * 256],
                      f"ln{st_}", pb[0][0:1, :256], pb[1][0:1, :256],
                      pb[2][:, :256], pb[3][:, :256])
        P.guard_set("x_t", f"ln{st_}v", nv)
    nxn = nv
    P.wait("tensor", "ln3v", nxn)

    for oc in range(48):
        par = oc % 2
        nw = phase_wstripe_T(P, w, oc * 128, wf, wb, wt[:, par], idb_sb, pbT, f"s{par}")
        P.wait("tensor", f"s{par}v", nw)
        g, slot = oc // 6, oc % 6
        is_q, is_k = slot < 4, slot == 4
        for tt in range(2):
            bank = pb[tt]
            P.guard_wait("tensor", f"pb{tt}")
            nt = None
            for kc in range(NKC):
                nt = P.op("tensor", lambda e, kc=kc, tt=tt, bank=bank, par=par: e.matmul(
                    bank[:, :], wt[:, par, kc, :], xn[:, kc, tt * 512:(tt + 1) * 512],
                    start=(kc == 0), stop=(kc == NKC - 1)), "pt")
            P.guard_set(f"s{par}_wtv", "pt", nt)
            P.wait("vector", "pt", nt)
            P.guard_wait("vector", f"praw{tt}")
            nv = P.op("vector", lambda e, tt=tt, oc=oc, bank=bank: e.tensor_scalar(
                praw[:, tt, :], bank[:, :], bq_sb[:, oc:oc + 1], None, ALU.add), "pv")
            P.guard_set(f"pb{tt}", "pv", nv)
            if is_q or is_k:
                P.wait("tensor", "pv", nv)
                P.guard_wait("tensor", "pbR")
                ntr = P.op("tensor", lambda e, tt=tt: e.matmul(
                    pbR[:, :], rot_sb[:, :], praw[:, tt, :], start=True, stop=True), "pt")
                P.wait("vector", "pt", ntr)
                cs = cos_sb[:, tt * 512:(tt + 1) * 512]
                sn = sin_sb[:, tt * 512:(tt + 1) * 512]
                P.op("vector", lambda e, tt=tt, cs=cs: e.tensor_tensor(
                    rope1[:, tt, :], praw[:, tt, :], cs, ALU.mult))
                nv2 = P.op("vector", lambda e, tt=tt, sn=sn: e.tensor_tensor(
                    praw[:, tt, :], pbR[:, :], sn, ALU.mult), "pv")
                P.guard_set("pbR", "pv", nv2)
                P.guard_set(f"praw{tt}", "pt", ntr)
                P.guard_wait("vector", f"obuf{tt}")
                nv3 = P.op("vector", lambda e, tt=tt: e.tensor_tensor(
                    obuf[:, tt, :], rope1[:, tt, :], praw[:, tt, :], ALU.add), "pv")
                P.wait("sync", "pv", nv3)
                if is_q:
                    nod = P.dma("sync", q_out[g * 4 + slot, :, tt * 512:(tt + 1) * 512],
                                obuf[:, tt, :], "od")
                else:
                    nod = P.dma("sync", k_out[g, :, tt * 512:(tt + 1) * 512],
                                obuf[:, tt, :], "od")
                P.guard_set(f"obuf{tt}", "od", nod)
            elif slot == 5:
                P.guard_wait("vector", f"obuf{tt}")
                nv2 = P.op("vector", lambda e, tt=tt: e.tensor_copy(
                    obuf[:, tt, :], praw[:, tt, :]), "pv")
                P.wait("tensor", "pv", nv2)
                P.guard_wait("tensor", "pbV")
                ntv = None
                for i in range(4):
                    ntv = P.op("tensor", lambda e, tt=tt, i=i: e.transpose(
                        pbV[:, i * 128:(i + 1) * 128],
                        obuf[:, tt, i * 128:(i + 1) * 128], idb_sb[:, :]), "pt")
                P.guard_set(f"obuf{tt}", "pt", ntv)
                P.wait("vector", "pt", ntv)
                P.guard_wait("vector", f"vtm{tt}")
                nv3 = P.op("vector", lambda e, tt=tt: e.tensor_copy(
                    vtm[:, tt, :, :].rearrange("p a b -> p (a b)"), pbV[:, :]), "pv")
                P.guard_set("pbV", "pv", nv3)
                P.wait("sync", "pv", nv3)
                nod = P.dma("sync",
                            v_out[g, tt * 512:(tt + 1) * 512, :].rearrange("(a p) b -> p a b", p=128),
                            vtm[:, tt, :, :], "od")
                P.guard_set(f"vtm{tt}", "od", nod)
    P.wait("sync", "od", P.cnt["od"])
    return P


def host_l2_inputs(c, inputs):
    ids = own_token_ids(c)
    x = np.asarray(inputs["L0_attn_hidden_states"], np.float32)[0]
    cos, sin = rope_tables()
    rot = np.zeros((128, 128), np.float32)
    for d in range(64):
        rot[d, 64 + d] = -1.0
        rot[64 + d, d] = 1.0
    return {
        "x_fm": fm(x[ids]).reshape(128, NKC, 1024),
        "w_qkv": np.asarray(inputs["L0_attn_w_qkv"], np.float32),
        "bqkv": np.ascontiguousarray(
            np.asarray(inputs["L0_attn_b_qkv"], np.float32).reshape(48, 128).T),
        "cos_own": np.ascontiguousarray(cos[ids].T),
        "sin_own": np.ascontiguousarray(sin[ids].T),
        "c_rot_T": np.ascontiguousarray(rot.T),
        "c_ident_bf": np.eye(128, dtype=np.float32).astype(BF),
        **common_host(),
    }



def build_l3_attn0():
    P = Prog()
    x = P.din("x_fm", [128, NKC, 1024], F32)
    q_in = P.din("q_fm", [32, 128, 1024], BF16)
    K = P.din("K_full", [8, 128, 8192], BF16)
    V = P.din("V_full", [8, 8192, 128], BF16)
    Ksh = P.din("K_sh", [8, 128, 8192], BF16)
    Vsh = P.din("V_sh", [8, 8192, 128], BF16)
    wd = P.din("w_dense", [4096, 4096], F32)
    bd = P.din("bdense", [128, 32], F32)
    mwin = P.din("m_win", [2, 128, 512], BF16)
    mstr = P.din("m_str", [16, 128, 448], BF16)
    cmask = P.din("c_causal", [128, 64], BF16)
    identb = P.din("c_ident_bf", [128, 128], BF16)
    x1 = P.dout("x1_fm", [128, NKC, 1024], F32)

    alloc_common(P)
    ctx = P.sbuf("ctx", [128, NH, 1024], BF16)
    q_sb = P.sbuf("q_sb", [128, 4, 1024], BF16)
    kwin = P.sbuf("kwin", [128, 1024], BF16)
    vwin = P.sbuf("vwin", [128, 8, 128], BF16)
    kstr = P.sbuf("kstr", [128, 14, 64], BF16)
    vstr = P.sbuf("vstr", [128, 7, 128], BF16)
    pw = P.sbuf("pw", [128, 512], BF16)
    pstr = P.sbuf("pstr", [128, 448], BF16)
    mw_sb = P.sbuf("mw_sb", [128, 2, 512], BF16)
    ms_sb = P.sbuf("ms_sb", [128, 16, 448], BF16)
    cm_sb = P.sbuf("cm_sb", [128, 64], BF16)
    idb_sb = P.sbuf("idb_sb", [128, 128], BF16)
    den = P.sbuf("den", [1, 4, 64], F32)
    den2 = P.sbuf("den2", [1, 64], F32)
    denr = P.sbuf("denr", [1, 256], F32)
    rb = P.sbuf("rb", [128, 256], F32)
    bd_sb = P.sbuf("bd_sb", [128, 32], F32)
    xsl = P.sbuf("xsl", [128, 2, 512], F32)
    xo = P.sbuf("xo", [128, 2, 512], F32)
    wf = P.sbuf("wf", [128, 4096], F32)
    wb = P.sbuf("wb", [128, 4096], BF16)
    af = wf[0:64, :]
    ab = wb[0:64, :]
    rf = wf[0:16, :]
    rb_ = wb[0:16, :]
    wt = P.sbuf("wt", [128, 2, NKC, 128], BF16)

    psS = P.psum("psS", [128, 512], F32)
    psR = P.psum("psR", [128, 512], F32)
    psC = P.psum("psC", [128, 512], F32)
    psD1 = P.psum("psD1", [1, 512], F32)
    psD2 = P.psum("psD2", [1, 512], F32)
    psB = P.psum("psB", [128, 512], F32)
    pbT = [P.psum(f"pbT{i}", [128, 1024], BF16) for i in range(2)]

    P.sem_new("c0", "ld", "kq", "ks", "kw", "pt", "pa", "pv", "od", "xd0", "xd1")
    n0 = load_common(P)
    P.dma("sync", mw_sb[:, :, :], mwin.rearrange("a p b -> p a b"), "ld")
    P.dma("sync", ms_sb[:, :, :], mstr.rearrange("a p b -> p a b"), "ld")
    P.dma("sync", cm_sb[:, :], cmask, "ld")
    P.dma("sync", bd_sb[:, :], bd, "ld")
    nld = P.dma("sync", idb_sb[:, :], identb, "ld")
    for e in ("scalar", "vector", "tensor"):
        P.wait(e, "ld", nld)
        P.wait(e, "c0", n0)

    SM = 1.0 / HD


    for g in range(8):
        P.guard_wait("sync", "q_sb")
        nkq = P.dma("sync", q_sb[:, :, :],
                    q_in.rearrange("a p b -> p a b")[:, 4 * g:4 * g + 4, :], "kq")
        P.guard_wait("sync", "kstr")
        P.dma("sync", kstr[:, :, :],
              K[g].rearrange("p (m i) -> p m i", i=64)[:, 7:119:8, :], "ks")
        P.guard_wait("sync", "vstr")
        P.dma("sync", vstr[0:64, :, :],
              V[g].rearrange("(m i) d -> i m d", i=64)[:, 7:112:16, :], "ks")
        nks = P.dma("sync", vstr[64:128, :, :],
                    V[g].rearrange("(m i) d -> i m d", i=64)[:, 15:120:16, :], "ks")
        P.wait("tensor", "kq", nkq)
        P.wait("tensor", "ks", nks)
        for j in range(16):
            npairs = l0_strided_pairs(j)
            P.guard_wait("sync", "kwin")
            if j < 2:
                P.dma("sync", kwin[:, :], K[g, :, 0:1024], "kw")
                nwd = P.dma("sync", vwin[:, :, :],
                            V[g, 0:1024, :].rearrange("(t p) d -> p t d", p=128), "kw")
            else:
                off = (8 * j - 15) * 64
                P.dma("sync", kwin[:, :], Ksh[g, :, off:off + 1024], "kw")
                nwd = P.dma("sync", vwin[:, :, :],
                            Vsh[g, off:off + 1024, :].rearrange("(t p) d -> p t d", p=128), "kw")
            P.wait("tensor", "kw", nwd)
            for hh in range(4):
                qa = q_sb[:, hh, j * 64:(j + 1) * 64]
                P.guard_wait("tensor", "psS")
                nt = None
                for w in range(8):
                    nt = P.op("tensor", lambda e, w=w, qa=qa: e.matmul(
                        psS[:, w * 64:(w + 1) * 64], kwin[:, w * 128:(w + 1) * 128],
                        qa, start=True, stop=True, skip_group_check=True), "pt")
                P.wait("scalar", "pt", nt)
                P.guard_wait("scalar", "pw")
                na = P.op("scalar", lambda e: e.activation(
                    pw[:, :], psS[:, :], AF.Exp, scale=SM), "pa")
                P.guard_set("psS", "pa", na)
                P.wait("vector", "pa", na)
                if j < 2:
                    nv = P.op("vector", lambda e, j=j: e.tensor_tensor(
                        pw[:, :], pw[:, :], mw_sb[:, j, :], ALU.mult), "pv")
                else:
                    nv = P.op("vector", lambda e: e.tensor_tensor(
                        pw[:, 448:512], pw[:, 448:512], cm_sb[:, :], ALU.mult), "pv")
                if npairs > 0:
                    P.guard_wait("tensor", "psR")
                    nt2 = None
                    for sp in range(npairs):
                        nt2 = P.op("tensor", lambda e, sp=sp, qa=qa: e.matmul(
                            psR[:, sp * 64:(sp + 1) * 64],
                            kstr[:, 2 * sp:2 * sp + 2, :].rearrange("p a b -> p (a b)"),
                            qa, start=True, stop=True, skip_group_check=True), "pt")
                    P.wait("scalar", "pt", nt2)
                    P.guard_wait("scalar", "pstr")
                    na2 = P.op("scalar", lambda e, npairs=npairs: e.activation(
                        pstr[:, :npairs * 64], psR[:, :npairs * 64], AF.Exp, scale=SM), "pa")
                    P.guard_set("psR", "pa", na2)
                    P.wait("vector", "pa", na2)
                    nv2 = P.op("vector", lambda e, j=j, npairs=npairs: e.tensor_tensor(
                        pstr[:, :npairs * 64], pstr[:, :npairs * 64],
                        ms_sb[:, j, :npairs * 64], ALU.mult), "pv")
                else:
                    nv2 = nv
                P.wait("tensor", "pv", max(nv, nv2))
                if hh == 0:
                    P.guard_wait("tensor", "psC")
                ntp = None
                npv = 8 + npairs
                for w in range(8):
                    ntp = P.op("tensor", lambda e, w=w, hh=hh, npv=npv: e.matmul(
                        psC[:, hh * 64:(hh + 1) * 64], vwin[:, w, :],
                        pw[:, w * 64:(w + 1) * 64],
                        start=(w == 0), stop=(npairs == 0 and w == 7),
                        skip_group_check=True), "pt")
                for sp in range(npairs):
                    ntp = P.op("tensor", lambda e, sp=sp, hh=hh: e.matmul(
                        psC[:, hh * 64:(hh + 1) * 64], vstr[:, sp, :],
                        pstr[:, sp * 64:(sp + 1) * 64],
                        start=False, stop=(sp == npairs - 1),
                        skip_group_check=True), "pt")
                P.guard_wait("tensor", "psD1")
                nt3 = P.op("tensor", lambda e: e.matmul(
                    psD1[:, :], P.sb["ones_bf"][:, :], pw[:, :], start=True, stop=True), "pt")
                if npairs > 0:
                    P.guard_wait("tensor", "psD2")
                    nt4 = P.op("tensor", lambda e, npairs=npairs: e.matmul(
                        psD2[:, :npairs * 64], P.sb["ones_bf"][:, :], pstr[:, :npairs * 64],
                        start=True, stop=True), "pt")
                else:
                    nt4 = nt3
                P.guard_set("pw", "pt", max(ntp, nt3))
                if npairs > 0:
                    P.guard_set("pstr", "pt", max(ntp, nt4))
                P.wait("vector", "pt", max(nt3, nt4))
                P.guard_wait("vector", "den")
                nv3 = P.op("vector", lambda e, hh=hh: e.tensor_reduce(
                    den[:, hh, :], psD1[:, :].rearrange("a (w q) -> a q w", q=64),
                    AXX, ALU.add), "pv")
                P.guard_set("psD1", "pv", nv3)
                if npairs > 0:
                    P.guard_wait("vector", "den2")
                    nv4 = P.op("vector", lambda e, npairs=npairs: e.tensor_reduce(
                        den2[:, :], psD2[:, :npairs * 64].rearrange("a (w q) -> a q w", q=64),
                        AXX, ALU.add), "pv")
                    P.guard_set("psD2", "pv", nv4)
                    nv4b = P.op("vector", lambda e, hh=hh: e.tensor_tensor(
                        den[:, hh, :], den[:, hh, :], den2[:, :], ALU.add), "pv")
                    P.guard_set("den2", "pv", nv4b)
            nlast = P.cnt["pt"]
            P.wait("vector", "pt", nlast)
            nv5 = P.op("vector", lambda e: e.reciprocal(denr[:, :], den[:, :, :].rearrange("a b c -> a (b c)")), "pv")
            P.wait("tensor", "pv", nv5)
            P.guard_wait("tensor", "psB")
            nt5 = P.op("tensor", lambda e: e.matmul(
                psB[:, 0:256], P.sb["ones1_f"][:, :], denr[:, :], start=True, stop=True), "pt")
            P.wait("vector", "pt", nt5)
            nv6 = P.op("vector", lambda e: e.tensor_copy(rb[:, :], psB[:, 0:256]), "pv")
            P.guard_set("psB", "pv", nv6)
            nv7 = P.op("vector", lambda e, g=g, j=j: e.tensor_tensor(
                ctx[:, 4 * g:4 * g + 4, j * 64:(j + 1) * 64],
                psC[:, 0:256].rearrange("p (h q) -> p h q", q=64),
                rb[:, :].rearrange("p (h q) -> p h q", q=64), ALU.mult), "pv")
            P.guard_set("psC", "pv", nv7)
            P.guard_set("kwin", "pt", P.cnt["pt"])
            P.guard_set("vwin", "pt", P.cnt["pt"])
        P.guard_set("q_sb", "pt", P.cnt["pt"])
        P.guard_set("kstr", "pt", P.cnt["pt"])
        P.guard_set("vstr", "pt", P.cnt["pt"])

    nctx = P.cnt["pv"]
    P.wait("tensor", "pv", nctx)
    for oc in range(NKC):
        par = oc % 2
        nw = phase_wstripe_T(P, wd, oc * 128, wf, wb, wt[:, par], idb_sb, pbT, f"s{par}")
        P.wait("tensor", f"s{par}v", nw)
        for tt in range(2):
            bank = psS if tt == 0 else psR
            bkey = "psS" if tt == 0 else "psR"
            P.guard_wait("tensor", bkey)
            nt = None
            for kc in range(NH):
                nt = P.op("tensor", lambda e, kc=kc, tt=tt, bank=bank, par=par: e.matmul(
                    bank[:, :], wt[:, par, kc, :], ctx[:, kc, tt * 512:(tt + 1) * 512],
                    start=(kc == 0), stop=(kc == NH - 1)), "pt")
            P.guard_set(f"s{par}_wtv", "pt", nt)
            P.guard_wait("sync", f"xsl{tt}")
            nxd = P.dma("sync", xsl[:, tt, :], x[:, oc, tt * 512:(tt + 1) * 512], f"xd{tt}")
            P.wait("vector", "pt", nt)
            P.wait("vector", f"xd{tt}", nxd)
            P.guard_wait("vector", f"xo{tt}")
            nv = P.op("vector", lambda e, oc=oc, tt=tt, bank=bank: e.scalar_tensor_tensor(
                xo[:, tt, :], bank[:, :], bd_sb[:, oc:oc + 1], xsl[:, tt, :],
                ALU.add, ALU.add), "pv")
            P.guard_set(bkey, "pv", nv)
            P.guard_set(f"xsl{tt}", "pv", nv)
            P.wait("sync", "pv", nv)
            nod = P.dma("sync", x1[:, oc, tt * 512:(tt + 1) * 512], xo[:, tt, :], "od")
            P.guard_set(f"xo{tt}", "od", nod)
    P.wait("sync", "od", P.cnt["od"])
    return P


def host_l3_inputs(c, inputs, x_fm, q_fm, K_full, V_full):
    mwin, mstr = build_l0_masks(c)
    sh = c * 64
    K_sh = np.zeros_like(K_full)
    K_sh[:, :, :SEQ - sh] = K_full[:, :, sh:]
    V_sh = np.zeros_like(V_full)
    V_sh[:, :SEQ - sh, :] = V_full[:, sh:, :]
    return {
        "x_fm": x_fm,
        "q_fm": q_fm,
        "K_full": K_full,
        "V_full": V_full,
        "K_sh": K_sh,
        "V_sh": V_sh,
        "w_dense": np.asarray(inputs["L0_attn_w_dense"], np.float32),
        "bdense": np.ascontiguousarray(
            np.asarray(inputs["L0_attn_b_dense"], np.float32).reshape(32, 128).T),
        "m_win": mwin.astype(BF),
        "m_str": mstr.astype(BF),
        "c_causal": np.concatenate([np.ones((64, 64), np.float32),
                                    causal_mask64()], axis=0).astype(BF),
        "c_ident_bf": np.eye(128, dtype=np.float32).astype(BF),
        **common_host(),
    }



def build_l4_ffn0():
    P = Prog()
    x = P.din("x_fm", [128, NKC, 1024], F32)
    WgT = P.din("WgT", [4096, I_FFN], BF16)
    WuT = P.din("WuT", [4096, I_FFN], BF16)
    WdT = P.din("WdT", [I_FFN, 4096], BF16)
    x2 = P.dout("x2_fm", [128, NKC, 1024], F32)
    h_dram = P.nc.dram_tensor("h_scr", [NIC, 128, 1024], BF16).ap()

    alloc_common(P)
    alloc_ln_scratch(P, 128)
    x_t = P.sbuf("x_t", [128, NKC, 128], F32)
    xn = P.sbuf("xn", [128, NKC, 1024], BF16)
    wgt = P.sbuf("wgt", [128, 2, NKC, 128], BF16)
    wut = P.sbuf("wut", [128, 2, NKC, 128], BF16)
    g_sb = P.sbuf("g_sb", [128, 512], BF16)
    u_sb = P.sbuf("u_sb", [128, 512], BF16)
    h_sb = P.sbuf("h_sb", [128, 2, 512], BF16)
    wdt = P.sbuf("wdt", [128, 2, 8, 512], BF16)
    h_ld = P.sbuf("h_ld", [128, 2, 8, 1024], BF16)
    xsl = P.sbuf("xsl", [128, 2, 512], F32)
    xo = P.sbuf("xo", [128, 2, 512], F32)

    pb = [P.psum(f"pb{i}", [128, 512], F32) for i in range(4)]
    pbG = P.psum("pbG", [128, 512], F32)
    pbU = P.psum("pbU", [128, 512], F32)
    pbD = P.psum("pbD", [128, 512], F32)
    pbB8 = P.psum("pbB8", [128, 512], F32)

    P.sem_new("c0", "xd", "wg0", "wg1", "wu0", "wu1", "pt", "pa", "pv",
              "hd", "ho0", "ho1", "wd0", "wd1", "hl0", "hl1", "xs0", "xs1", "od")
    n0 = load_common(P)
    for e in ("scalar", "vector", "tensor"):
        P.wait(e, "c0", n0)

    for st_ in range(8):
        P.guard_wait("sync", "x_t")
        nxd = P.dma("sync", x_t[:, :, :], x[:, :, st_ * 128:(st_ + 1) * 128], "xd")
        for e in ("scalar", "tensor", "vector"):
            P.wait(e, "xd", nxd)
        nv = phase_ln(P, x_t[:, :, :], 128, xn[:, :, st_ * 128:(st_ + 1) * 128],
                      f"ln{st_}", pb[0][0:1, :128], pb[1][0:1, :128],
                      pb[2][:, :128], pb[3][:, :128])
        P.guard_set("x_t", f"ln{st_}v", nv)
    P.wait("tensor", "ln7v", P.cnt["ln7v"])

    for ic in range(NIC):
        par = ic % 2
        P.guard_wait("sync", f"wgt{par}")
        nwg = P.dma("sync", wgt[:, par],
                    WgT[:, ic * 128:(ic + 1) * 128].rearrange("(a p) b -> p a b", p=128),
                    f"wg{par}")
        P.guard_wait("sync", f"wut{par}")
        nwu = P.dma("sync", wut[:, par],
                    WuT[:, ic * 128:(ic + 1) * 128].rearrange("(a p) b -> p a b", p=128),
                    f"wu{par}")
        P.wait("tensor", f"wg{par}", nwg)
        P.wait("tensor", f"wu{par}", nwu)
        for tt in range(2):
            P.guard_wait("tensor", "pbG")
            ntg = None
            for kc in range(NKC):
                ntg = P.op("tensor", lambda e, kc=kc, tt=tt, par=par: e.matmul(
                    pbG[:, :], wgt[:, par, kc, :], xn[:, kc, tt * 512:(tt + 1) * 512],
                    start=(kc == 0), stop=(kc == NKC - 1)), "pt")
            P.guard_wait("tensor", "pbU")
            ntu = None
            for kc in range(NKC):
                ntu = P.op("tensor", lambda e, kc=kc, tt=tt, par=par: e.matmul(
                    pbU[:, :], wut[:, par, kc, :], xn[:, kc, tt * 512:(tt + 1) * 512],
                    start=(kc == 0), stop=(kc == NKC - 1)), "pt")
            P.guard_set(f"wgt{par}", "pt", ntg)
            P.guard_set(f"wut{par}", "pt", ntu)
            P.wait("scalar", "pt", ntg)
            P.guard_wait("scalar", "g_sb")
            na = P.op("scalar", lambda e: e.activation(g_sb[:, :], pbG[:, :], AF.Silu), "pa")
            P.guard_set("pbG", "pa", na)
            P.wait("vector", "pt", ntu)
            P.guard_wait("vector", "u_sb")
            nv = P.op("vector", lambda e: e.tensor_copy(u_sb[:, :], pbU[:, :]), "pv")
            P.guard_set("pbU", "pv", nv)
            P.wait("vector", "pa", na)
            P.guard_wait("vector", f"h_sb{tt}")
            nv2 = P.op("vector", lambda e, tt=tt: e.tensor_tensor(
                h_sb[:, tt, :], g_sb[:, :], u_sb[:, :], ALU.mult), "pv")
            P.guard_set("g_sb", "pv", nv2)
            P.guard_set("u_sb", "pv", nv2)
            P.wait("sync", "pv", nv2)
            nho = P.dma("sync", h_dram[ic, :, tt * 512:(tt + 1) * 512], h_sb[:, tt, :],
                        f"ho{tt}")
            P.guard_set(f"h_sb{tt}", f"ho{tt}", nho)
    nh_all0, nh_all1 = P.cnt["ho0"], P.cnt["ho1"]

    sc_list = []
    icx = 0
    while icx < NIC:
        sc_list.append((icx, min(8, NIC - icx)))
        icx += min(8, NIC - icx)
    P.wait("sync", "ho0", nh_all0)
    P.wait("sync", "ho1", nh_all1)
    P.wait("tensor", "pa", P.cnt["pa"])
    P.wait("tensor", "pv", P.cnt["pv"])
    pbDD = [pb[0], pb[1], pb[2], pb[3], pbG, pbU, pbD, pbB8]
    for og in range(8):
        for bi in range(8):
            P.guard_wait("tensor", f"pbDD{bi}")
        for si, (ic0, nic8) in enumerate(sc_list):
            par = si % 2
            P.guard_wait("sync", f"h_ld{par}")
            nhl = P.dma("sync", h_ld[:, par, :nic8],
                        h_dram[ic0:ic0 + nic8].rearrange("a p b -> p a b"), f"hl{par}")
            P.guard_wait("sync", f"wdt{par}")
            nwd = P.dma("sync", wdt[:, par, :nic8],
                        WdT[ic0 * 128:(ic0 + nic8) * 128, og * 512:(og + 1) * 512].rearrange(
                            "(a p) b -> p a b", p=128), f"wd{par}")
            P.wait("tensor", f"wd{par}", nwd)
            P.wait("tensor", f"hl{par}", nhl)
            for oi in range(4):
                for i8 in range(nic8):
                    for tt in range(2):
                        first = (si == 0 and i8 == 0)
                        last = (si == len(sc_list) - 1 and i8 == nic8 - 1)
                        bi = oi * 2 + tt
                        nt = P.op("tensor", lambda e, i8=i8, tt=tt, par=par, oi=oi,
                                  first=first, last=last, bi=bi: e.matmul(
                            pbDD[bi][:, :], wdt[:, par, i8, oi * 128:(oi + 1) * 128],
                            h_ld[:, par, i8, tt * 512:(tt + 1) * 512],
                            start=first, stop=last, skip_group_check=True), "pt")
            P.guard_set(f"wdt{par}", "pt", P.cnt["pt"])
            P.guard_set(f"h_ld{par}", "pt", P.cnt["pt"])
        for oi in range(4):
            oc = og * 4 + oi
            for tt in range(2):
                bi = oi * 2 + tt
                P.guard_wait("sync", f"xsl{tt}")
                nxs = P.dma("sync", xsl[:, tt, :], x[:, oc, tt * 512:(tt + 1) * 512], f"xs{tt}")
                P.wait("vector", "pt", P.cnt["pt"])
                P.wait("vector", f"xs{tt}", nxs)
                P.guard_wait("vector", f"xo{tt}")
                nv = P.op("vector", lambda e, tt=tt, bi=bi: e.tensor_tensor(
                    xo[:, tt, :], pbDD[bi][:, :], xsl[:, tt, :], ALU.add), "pv")
                P.guard_set(f"pbDD{bi}", "pv", nv)
                P.guard_set(f"xsl{tt}", "pv", nv)
                P.wait("sync", "pv", nv)
                nod = P.dma("sync", x2[:, oc, tt * 512:(tt + 1) * 512], xo[:, tt, :], "od")
                P.guard_set(f"xo{tt}", "od", nod)
    P.wait("sync", "od", P.cnt["od"])
    return P


def host_l4_inputs(c, x1_fm, WgT, WuT, WdT):
    return {"x_fm": x1_fm, "WgT": WgT, "WuT": WuT, "WdT": WdT, **common_host()}



def build_l5_qkv1():
    P = Prog()
    x = P.din("x_fm", [128, NKC, 1024], F32)
    wq = P.din("wq", [4096, 4096], F32)
    wk = P.din("wk", [4096, 128], F32)
    wv = P.din("wv", [4096, 128], F32)
    identb = P.din("c_ident_bf", [128, 128], BF16)
    q_out = P.dout("q1_fm", [32, 128, 1024], BF16)
    k_out = P.dout("k1_fm", [128, 1024], BF16)
    v_out = P.dout("v1_tm", [1024, 128], BF16)

    alloc_common(P)
    alloc_ln_scratch(P, 256)
    x_t = P.sbuf("x_t", [128, NKC, 256], F32)
    xn = P.sbuf("xn", [128, NKC, 1024], BF16)
    wt = P.sbuf("wt", [128, 2, NKC, 128], BF16)
    wkv = P.sbuf("wkv", [128, 2, NKC, 128], BF16)
    obuf = P.sbuf("obuf", [128, 2, 512], BF16)
    vtm = P.sbuf("vtm", [128, 2, 4, 128], BF16)
    idb_sb = P.sbuf("idb_sb", [128, 128], BF16)

    pb = [P.psum(f"pb{i}", [128, 512], F32) for i in range(4)]
    pbV = P.psum("pbV", [128, 512], BF16)

    P.sem_new("c0", "ld", "xd", "wq0", "wq1", "wkv0", "wkv1", "pt", "pv", "od")
    n0 = load_common(P)
    nld = P.dma("sync", idb_sb[:, :], identb, "ld")
    for e in ("scalar", "vector", "tensor"):
        P.wait(e, "c0", n0)
        P.wait(e, "ld", nld)

    for st_ in range(4):
        P.guard_wait("sync", "x_t")
        nxd = P.dma("sync", x_t[:, :, :], x[:, :, st_ * 256:(st_ + 1) * 256], "xd")
        for e in ("scalar", "tensor", "vector"):
            P.wait(e, "xd", nxd)
        nv = phase_ln(P, x_t[:, :, :], 256, xn[:, :, st_ * 256:(st_ + 1) * 256],
                      f"ln{st_}", pb[0][0:1, :256], pb[1][0:1, :256],
                      pb[2][:, :256], pb[3][:, :256])
        P.guard_set("x_t", f"ln{st_}v", nv)
    P.wait("tensor", "ln3v", P.cnt["ln3v"])

    for oc in range(NKC):
        par = oc % 2
        P.guard_wait("gpsimd", f"wt{par}")
        nw = P.dma("gpsimd", wt[:, par],
                   wq[:, oc * 128:(oc + 1) * 128].rearrange("(a p) b -> p a b", p=128),
                   f"wq{par}")
        P.wait("tensor", f"wq{par}", nw)
        for tt in range(2):
            bank = pb[tt]
            P.guard_wait("tensor", f"pb{tt}")
            nt = None
            for kc in range(NKC):
                nt = P.op("tensor", lambda e, kc=kc, tt=tt, par=par, bank=bank: e.matmul(
                    bank[:, :], wt[:, par, kc, :], xn[:, kc, tt * 512:(tt + 1) * 512],
                    start=(kc == 0), stop=(kc == NKC - 1)), "pt")
            P.guard_set(f"wt{par}", "pt", nt)
            P.wait("vector", "pt", nt)
            P.guard_wait("vector", f"obuf{tt}")
            nv = P.op("vector", lambda e, tt=tt, bank=bank: e.tensor_copy(
                obuf[:, tt, :], bank[:, :]), "pv")
            P.guard_set(f"pb{tt}", "pv", nv)
            P.wait("sync", "pv", nv)
            nod = P.dma("sync", q_out[oc, :, tt * 512:(tt + 1) * 512], obuf[:, tt, :], "od")
            P.guard_set(f"obuf{tt}", "od", nod)

    for which, wsrc, who in (("k", wk, 0), ("v", wv, 1)):
        par = who
        P.guard_wait("gpsimd", f"wkv{par}")
        nw = P.dma("gpsimd", wkv[:, par],
                   wsrc.rearrange("(a p) b -> p a b", p=128), f"wkv{par}")
        P.wait("tensor", f"wkv{par}", nw)
        for tt in range(2):
            bank = pb[2 + tt]
            P.guard_wait("tensor", f"pb{2 + tt}")
            nt = None
            for kc in range(NKC):
                nt = P.op("tensor", lambda e, kc=kc, tt=tt, par=par, bank=bank: e.matmul(
                    bank[:, :], wkv[:, par, kc, :], xn[:, kc, tt * 512:(tt + 1) * 512],
                    start=(kc == 0), stop=(kc == NKC - 1)), "pt")
            P.guard_set(f"wkv{par}", "pt", nt)
            P.wait("vector", "pt", nt)
            P.guard_wait("vector", f"obuf{tt}")
            nv = P.op("vector", lambda e, tt=tt, bank=bank: e.tensor_copy(
                obuf[:, tt, :], bank[:, :]), "pv")
            P.guard_set(f"pb{2 + tt}", "pv", nv)
            if which == "k":
                P.wait("sync", "pv", nv)
                nod = P.dma("sync", k_out[:, tt * 512:(tt + 1) * 512], obuf[:, tt, :], "od")
                P.guard_set(f"obuf{tt}", "od", nod)
            else:
                P.wait("tensor", "pv", nv)
                P.guard_wait("tensor", "pbV")
                ntv = None
                for i in range(4):
                    ntv = P.op("tensor", lambda e, tt=tt, i=i: e.transpose(
                        pbV[:, i * 128:(i + 1) * 128],
                        obuf[:, tt, i * 128:(i + 1) * 128], idb_sb[:, :]), "pt")
                P.guard_set(f"obuf{tt}", "pt", ntv)
                P.wait("vector", "pt", ntv)
                P.guard_wait("vector", f"vtm{tt}")
                nv2 = P.op("vector", lambda e, tt=tt: e.tensor_copy(
                    vtm[:, tt, :, :].rearrange("p a b -> p (a b)"), pbV[:, :]), "pv")
                P.guard_set("pbV", "pv", nv2)
                P.wait("sync", "pv", nv2)
                nod = P.dma("sync",
                            v_out[tt * 512:(tt + 1) * 512, :].rearrange("(a p) b -> p a b", p=128),
                            vtm[:, tt, :, :], "od")
                P.guard_set(f"vtm{tt}", "od", nod)
    P.wait("sync", "od", P.cnt["od"])
    return P


def host_l5_inputs(c, inputs, x2_fm):
    return {
        "x_fm": x2_fm,
        "wq": np.asarray(inputs["L1_attn_wq"], np.float32),
        "wk": np.asarray(inputs["L1_attn_wk"], np.float32),
        "wv": np.asarray(inputs["L1_attn_wv"], np.float32),
        "c_ident_bf": np.eye(128, dtype=np.float32).astype(BF),
        **common_host(),
    }



def build_l6_attn1():
    P = Prog()
    x = P.din("x_fm", [128, NKC, 1024], F32)
    q_in = P.din("q1_fm", [32, 128, 1024], BF16)
    K1 = P.din("K1", [128, 8192], BF16)
    V1 = P.din("V1", [8192, 128], BF16)
    wo = P.din("wo", [4096, 4096], F32)
    m1 = P.din("m_l1", [64, 128, 128], BF16)
    x3 = P.dout("x3_fm", [128, NKC, 1024], F32)

    alloc_common(P)
    ctx = P.sbuf("ctx", [128, NH, 1024], BF16)
    k_sb = P.sbuf("k_sb", [128, 64, 128], BF16)
    v_sb = P.sbuf("v_sb", [128, 64, 128], BF16)
    q_sb = P.sbuf("q_sb", [128, 2, 1024], BF16)
    m_sb = P.sbuf("m_sb", [128, 64, 128], BF16)
    pw = P.sbuf("pw", [128, 2, 2, 512], BF16)
    den = P.sbuf("den", [1, 1024], F32)
    denr = P.sbuf("denr", [1, 1024], F32)
    rb = P.sbuf("rb", [128, 2, 512], F32)
    wt = P.sbuf("wt", [128, 2, NKC, 128], BF16)
    xsl = P.sbuf("xsl", [128, 2, 512], F32)
    xo = P.sbuf("xo", [128, 2, 512], F32)

    psS = [P.psum(f"psS{i}", [128, 512], F32) for i in range(2)]
    psC = [P.psum(f"psC{i}", [128, 512], F32) for i in range(2)]
    psD = [P.psum(f"psD{i}", [1, 512], F32) for i in range(2)]
    psB = P.psum("psB", [128, 512], F32)
    SM = float(1.0 / np.sqrt(128.0))

    P.sem_new("c0", "ld", "qd0", "qd1", "pt", "pa", "pv", "od0", "od1", "xs0", "xs1",
              "wo0", "wo1")
    n0 = load_common(P)
    P.dma("sync", k_sb[:, :, :], K1.rearrange("p (a b) -> p a b", b=128), "ld")
    P.dma("sync", v_sb[:, :, :], V1.rearrange("(a p) b -> p a b", p=128), "ld")
    nld = P.dma("sync", m_sb[:, :, :], m1.rearrange("a p b -> p a b"), "ld")
    for e in ("scalar", "vector", "tensor"):
        P.wait(e, "c0", n0)
        P.wait(e, "ld", nld)

    ops_struct = []
    for t in range(64):
        j0 = l1_j0u(t)
        col0 = j0 * 64
        cc = col0
        ci = 0
        while cc < 1024:
            w = min(512, 1024 - cc)
            for half in range(2):
                h0, h1 = half * 512, (half + 1) * 512
                lo, hi = max(cc, h0), min(cc + w, h1)
                if lo < hi:
                    ops_struct.append((t, ci, cc, w, half, lo, hi))
            cc += w
            ci += 1
    npv = [sum(1 for o in ops_struct if o[4] == h) for h in range(2)]

    for qh in range(NH):
        par = qh % 2
        P.guard_wait("sync", f"q_sb{par}")
        nqd = P.dma("sync", q_sb[:, par, :], q_in[qh], f"qd{par}")
        P.wait("tensor", f"qd{par}", nqd)
        pvk = [0, 0]

        def chunks_of(t):
            j0 = l1_j0u(t)
            col0 = j0 * 64
            ch = []
            cc = col0
            while cc < 1024:
                w = min(512, 1024 - cc)
                ch.append((cc, w))
                cc += w
            return ch

        def emit_scores(t):
            tp = t % 2
            chunks = chunks_of(t)
            ncols = 1024 - chunks[0][0]
            nts, nas = [], []
            for ci, (cc, w) in enumerate(chunks):
                bank = psS[ci % 2]
                P.guard_wait("tensor", f"psS{ci % 2}")
                nt = P.op("tensor", lambda e, t=t, cc=cc, w=w, par=par, bank=bank: e.matmul(
                    bank[:, :w], k_sb[:, t, :], q_sb[:, par, cc:cc + w],
                    start=True, stop=True, skip_group_check=True), "pt")
                nts.append(nt)
                P.wait("scalar", "pt", nt)
                P.guard_wait("scalar", f"pw{tp}{ci % 2}")
                na = P.op("scalar", lambda e, ci=ci, w=w, tp=tp, bank=bank: e.activation(
                    pw[:, tp, ci % 2, :w], bank[:, :w], AF.Exp, scale=SM), "pa")
                P.guard_set(f"psS{ci % 2}", "pa", na)
                nas.append(na)
            P.wait("vector", "pa", nas[0])
            mw = min(128, ncols)
            nv = P.op("vector", lambda e, t=t, mw=mw, tp=tp: e.tensor_tensor(
                pw[:, tp, 0, :mw], pw[:, tp, 0, :mw], m_sb[:, t, :mw], ALU.mult), "pv")
            return (t, chunks, nas, nv)

        def emit_pv(info):
            t, chunks, nas, nv = info
            tp = t % 2
            for ci, (cc, w) in enumerate(chunks):
                if ci == 0:
                    P.wait("tensor", "pv", nv)
                else:
                    P.wait("tensor", "pa", nas[ci])
                for half in range(2):
                    h0, h1 = half * 512, (half + 1) * 512
                    lo, hi = max(cc, h0), min(cc + w, h1)
                    if lo >= hi:
                        continue
                    fc = (pvk[half] == 0)
                    lastf = (pvk[half] == npv[half] - 1)
                    pvk[half] += 1
                    if fc:
                        P.guard_wait("tensor", f"psC{half}")
                        P.guard_wait("tensor", f"psD{half}")
                    ntp = P.op("tensor", lambda e, t=t, lo=lo, hi=hi, half=half, ci=ci,
                               fc=fc, lastf=lastf, cc=cc, tp=tp: e.matmul(
                        psC[half][:, lo - half * 512:hi - half * 512], v_sb[:, t, :],
                        pw[:, tp, ci % 2, lo - cc:hi - cc],
                        start=fc, stop=lastf, skip_group_check=True), "pt")
                    ntd = P.op("tensor", lambda e, lo=lo, hi=hi, ci=ci, cc=cc, half=half,
                               fc=fc, lastf=lastf, tp=tp: e.matmul(
                        psD[half][:, lo - half * 512:hi - half * 512],
                        P.sb["ones_bf"][:, :], pw[:, tp, ci % 2, lo - cc:hi - cc],
                        start=fc, stop=lastf, skip_group_check=True), "pt")
                    P.guard_set(f"pw{tp}{ci % 2}", "pt", max(ntp, ntd))

        prev = None
        for t in range(64):
            info = emit_scores(t)
            if prev is not None:
                emit_pv(prev)
            prev = info
        emit_pv(prev)
        nlast = P.cnt["pt"]
        P.wait("vector", "pt", nlast)
        P.guard_wait("vector", "den")
        P.op("vector", lambda e: e.tensor_copy(den[:, 0:512], psD[0][:, :]), "pv")
        nv5 = P.op("vector", lambda e: e.tensor_copy(den[:, 512:1024], psD[1][:, :]), "pv")
        P.guard_set("psD0", "pv", nv5)
        P.guard_set("psD1", "pv", nv5)
        P.wait("vector", "pv", nv5)
        nv6 = P.op("vector", lambda e: e.reciprocal(denr[:, :], den[:, :]), "pv")
        P.wait("tensor", "pv", nv6)
        for half in range(2):
            P.guard_wait("tensor", "psB")
            nt5 = P.op("tensor", lambda e, half=half: e.matmul(
                psB[:, :], P.sb["ones1_f"][:, :], denr[:, half * 512:(half + 1) * 512],
                start=True, stop=True, skip_group_check=True), "pt")
            P.wait("vector", "pt", nt5)
            P.guard_wait("vector", f"rb{half}")
            nv7 = P.op("vector", lambda e, half=half: e.tensor_copy(
                rb[:, half, :], psB[:, :]), "pv")
            P.guard_set("psB", "pv", nv7)
            P.wait("vector", "pv", nv7)
            nv8 = P.op("vector", lambda e, qh=qh, half=half: e.tensor_tensor(
                ctx[:, qh, half * 512:(half + 1) * 512], psC[half][:, :],
                rb[:, half, :], ALU.mult), "pv")
            P.guard_set(f"psC{half}", "pv", nv8)
            P.guard_set(f"rb{half}", "pv", nv8)
        P.guard_set(f"q_sb{par}", "pt", P.cnt["pt"])

    nctx = P.cnt["pv"]
    P.wait("tensor", "pv", nctx)
    for oc in range(NKC):
        par = oc % 2
        P.guard_wait("gpsimd", f"wt{par}")
        nw = P.dma("gpsimd", wt[:, par],
                   wo[:, oc * 128:(oc + 1) * 128].rearrange("(a p) b -> p a b", p=128),
                   f"wo{par}")
        P.wait("tensor", f"wo{par}", nw)
        for tt in range(2):
            bank = psS[tt]
            P.guard_wait("tensor", f"psS{tt}")
            nt = None
            for kc in range(NKC):
                nt = P.op("tensor", lambda e, kc=kc, tt=tt, par=par, bank=bank: e.matmul(
                    bank[:, :], wt[:, par, kc, :], ctx[:, kc, tt * 512:(tt + 1) * 512],
                    start=(kc == 0), stop=(kc == NKC - 1)), "pt")
            P.guard_set(f"wt{par}", "pt", nt)
            P.guard_wait("sync", f"xsl{tt}")
            nxs = P.dma("sync", xsl[:, tt, :], x[:, oc, tt * 512:(tt + 1) * 512], f"xs{tt}")
            P.wait("vector", "pt", nt)
            P.wait("vector", f"xs{tt}", nxs)
            P.guard_wait("vector", f"xo{tt}")
            nv = P.op("vector", lambda e, tt=tt, bank=bank: e.tensor_tensor(
                xo[:, tt, :], bank[:, :], xsl[:, tt, :], ALU.add), "pv")
            P.guard_set(f"psS{tt}", "pv", nv)
            P.guard_set(f"xsl{tt}", "pv", nv)
            P.wait("sync", "pv", nv)
            nod = P.dma("sync", x3[:, oc, tt * 512:(tt + 1) * 512], xo[:, tt, :], f"od{tt}")
            P.guard_set(f"xo{tt}", f"od{tt}", nod)
    P.wait("sync", "od0", P.cnt["od0"])
    P.wait("sync", "od1", P.cnt["od1"])
    return P


def host_l6_inputs(c, inputs, x2_fm, q1_fm, K1, V1):
    return {
        "x_fm": x2_fm,
        "q1_fm": q1_fm,
        "K1": K1,
        "V1": V1,
        "wo": np.asarray(inputs["L1_attn_wo"], np.float32),
        "m_l1": build_l1_masks(c).astype(BF),
        **common_host(),
    }



def build_l7_ffn1():
    P = Prog()
    x = P.din("x_fm", [128, NKC, 1024], F32)
    bw = P.din("base_w", [4096, 4096], F32)
    bb = P.din("base_b", [128, 32], F32)
    A = P.din("A_w", [64, 4096], F32)
    Bm = P.din("B_w", [16, 4096, 64], F32)
    rw = P.din("router_w", [16, 4096], F32)
    identb = P.din("c_ident_bf", [128, 128], BF16)
    x4 = P.dout("x4_fm", [128, NKC, 1024], F32)

    alloc_common(P)
    alloc_ln_scratch(P, 256)
    x_t = P.sbuf("x_t", [128, NKC, 256], F32)
    xn = P.sbuf("xn", [128, NKC, 1024], BF16)
    at_sb = P.sbuf("at_sb", [128, NKC, 64], BF16)
    rt_sb = P.sbuf("rt_sb", [128, NKC, 16], BF16)
    bf_ = P.sbuf("bf_", [128, 16, 64], F32)
    bb16 = P.sbuf("bb16", [128, 16, 64], BF16)
    br_sb = P.sbuf("br_sb", [128, 2, 8, 128], BF16)
    brw = P.sbuf("brw", [128, 128], BF16)
    logit = P.sbuf("logit", [128, 8, 16], F32)
    rh_sb = P.sbuf("rh_sb", [128, 8, 64], F32)
    tk = P.sbuf("tk", [128, 10, 16], F32)
    tks = P.sbuf("tks", [128, 8, 1], F32)
    comb = P.sbuf("comb", [128, 16], F32)
    C_sb = P.sbuf("C_sb", [128, 1024], BF16)
    ct_sb = P.sbuf("ct_sb", [128, 8, 1024], BF16)
    wf = P.sbuf("wf", [128, 4096], F32)
    wb = P.sbuf("wb", [128, 4096], BF16)
    af = wf[0:64, :]
    ab = wb[0:64, :]
    rf = wf[0:16, :]
    rb_ = wb[0:16, :]
    wt = P.sbuf("wt", [128, 2, NKC, 128], BF16)
    bb_sb = P.sbuf("bb_sb", [128, 32], F32)
    idb_sb = P.sbuf("idb_sb", [128, 128], BF16)
    xsl = P.sbuf("xsl", [128, 2, 512], F32)
    xo = P.sbuf("xo", [128, 2, 512], F32)

    pb = [P.psum(f"pb{i}", [128, 512], F32) for i in range(4)]
    pbT = [P.psum(f"pbT{i}", [128, 1024], BF16) for i in range(2)]
    pbL = P.psum("pbL", [128, 512], F32)
    br_dram = P.nc.dram_tensor("br_scr", [1024, 4096], BF16).ap()

    P.sem_new("c0", "ld", "xd", "pp", "pa", "pv", "pt", "od", "xs0", "xs1", "brd0", "brd1")
    n0 = load_common(P)
    P.dma("sync", af, A, "ld")
    P.dma("sync", bb_sb[:, :], bb, "ld")
    nld = P.dma("sync", idb_sb[:, :], identb, "ld")
    for e in ("scalar", "vector", "tensor"):
        P.wait(e, "c0", n0)
        P.wait(e, "ld", nld)

    na = P.op("scalar", lambda e: e.activation(ab, af, AF.Copy, bias=0.0), "pa")
    P.wait("tensor", "pa", na)
    for kc in range(NKC):
        bank = pbT[kc % 2]
        P.guard_wait("tensor", f"pbT{kc % 2}")
        nt = P.op("tensor", lambda e, kc=kc, bank=bank: e.transpose(
            bank[:, 0:64], ab[:, kc * 128:(kc + 1) * 128], idb_sb[0:64, 0:64]), "pt")
        P.wait("vector", "pt", nt)
        nv = P.op("vector", lambda e, kc=kc, bank=bank: e.tensor_copy(at_sb[:, kc, :], bank[:, 0:64]), "pv")
        P.guard_set(f"pbT{kc % 2}", "pv", nv)
    nlastA = P.cnt["pt"]
    P.wait("sync", "pt", nlastA)
    nrd = P.dma("sync", rf, rw, "ld")
    P.wait("scalar", "ld", nrd)
    P.wait("scalar", "pt", nlastA)
    na = P.op("scalar", lambda e: e.activation(rb_, rf, AF.Copy, bias=0.0), "pa")
    P.wait("tensor", "pa", na)
    for kc in range(NKC):
        bank = pbT[kc % 2]
        P.guard_wait("tensor", f"pbT{kc % 2}")
        nt = P.op("tensor", lambda e, kc=kc, bank=bank: e.transpose(
            bank[:, 0:16], rb_[:, kc * 128:(kc + 1) * 128], idb_sb[0:16, 0:16]), "pt")
        P.wait("vector", "pt", nt)
        nv = P.op("vector", lambda e, kc=kc, bank=bank: e.tensor_copy(rt_sb[:, kc, :], bank[:, 0:16]), "pv")
        P.guard_set(f"pbT{kc % 2}", "pv", nv)
    P.guard_set("s0_wf0", "pt", P.cnt["pt"])
    P.guard_set("s0_wf1", "pt", P.cnt["pt"])
    P.guard_set("s1_wf0", "pt", P.cnt["pt"])
    P.guard_set("s1_wf1", "pt", P.cnt["pt"])
    P.guard_set("s0_wb", "pt", P.cnt["pt"])
    P.guard_set("s1_wb", "pt", P.cnt["pt"])
    P.wait("tensor", "pv", P.cnt["pv"])
    for ee in range(16):
        for bh in range(2):
            P.guard_wait("sync", "bf_")
            nbd = P.dma("sync", bf_[:, :, :],
                        Bm[ee, bh * 2048:(bh + 1) * 2048, :].rearrange("(a p) b -> p a b", p=128),
                        "xd")
            P.wait("scalar", "xd", nbd)
            P.guard_wait("scalar", "bb16")
            na = P.op("scalar", lambda e: e.activation(bb16[:, :, :], bf_[:, :, :], AF.Copy, bias=0.0), "pa")
            P.guard_set("bf_", "pa", na)
            P.wait("tensor", "pa", na)
            for dl in range(16):
                dt = bh * 16 + dl
                bank = pbT[dt % 2]
                P.guard_wait("tensor", f"pbT{dt % 2}x")
                nt = P.op("tensor", lambda e, dl=dl, bank=bank: e.transpose(
                    bank[0:64, 0:128], bb16[:, dl, :].rearrange("p (a b) -> p a b", a=1)[:, 0, :],
                    idb_sb[:, :]), "pt")
                P.wait("vector", "pt", nt)
                P.guard_wait("vector", "brw")
                nv = P.op("vector", lambda e, bank=bank: e.tensor_copy(
                    brw[0:64, :], bank[0:64, 0:128]), "pv")
                P.guard_set(f"pbT{dt % 2}x", "pv", nv)
                P.wait("sync", "pv", nv)
                nbo = P.dma("sync", br_dram[ee * 64:(ee + 1) * 64, dt * 128:(dt + 1) * 128],
                            brw[0:64, :], "od")
                P.guard_set("brw", "od", nbo)
            P.guard_set("bb16", "pt", P.cnt["pt"])
    npr = P.cnt["pv"]

    for st_ in range(4):
        P.guard_wait("sync", "x_t")
        nxd = P.dma("sync", x_t[:, :, :], x[:, :, st_ * 256:(st_ + 1) * 256], "xd")
        for e in ("scalar", "tensor", "vector"):
            P.wait(e, "xd", nxd)
        nv = phase_ln(P, x_t[:, :, :], 256, xn[:, :, st_ * 256:(st_ + 1) * 256],
                      f"ln{st_}", pb[0][0:1, :256], pb[1][0:1, :256],
                      pb[2][:, :256], pb[3][:, :256])
        P.guard_set("x_t", f"ln{st_}v", nv)
    P.wait("tensor", "ln3v", P.cnt["ln3v"])

    for ti in range(8):
        P.guard_wait("tensor", "pbL")
        for kc in range(NKC):
            P.op("tensor", lambda e, kc=kc, ti=ti: e.matmul(
                pbL[:, 0:16], xn[:, kc, ti * 128:(ti + 1) * 128], rt_sb[:, kc, :],
                start=(kc == 0), stop=(kc == NKC - 1), skip_group_check=True), "pt")
            nt = P.op("tensor", lambda e, kc=kc, ti=ti: e.matmul(
                pbL[:, 16:80], xn[:, kc, ti * 128:(ti + 1) * 128], at_sb[:, kc, :],
                start=(kc == 0), stop=(kc == NKC - 1), skip_group_check=True), "pt")
        P.wait("vector", "pt", nt)
        P.op("vector", lambda e, ti=ti: e.tensor_copy(logit[:, ti, :], pbL[:, 0:16]))
        nv = P.op("vector", lambda e, ti=ti: e.tensor_copy(rh_sb[:, ti, :], pbL[:, 16:80]), "pv")
        P.guard_set("pbL", "pv", nv)
        nv = P.op("vector", lambda e, ti=ti: e.reduce_max(tks[:, 0, :], logit[:, ti, :], AXX, negate=True), "pv")
        P.wait("scalar", "pv", nv)
        na = P.op("scalar", lambda e, ti=ti: e.activation(
            tk[:, 0, :], logit[:, ti, :], AF.Exp, bias=tks[:, 0, :], scale=1.0), "pa")
        P.wait("vector", "pa", na)
        P.op("vector", lambda e: e.reduce_sum(tks[:, 1, :], tk[:, 0, :], AXX))
        P.op("vector", lambda e: e.reciprocal(tks[:, 2, :], tks[:, 1, :]))
        P.op("vector", lambda e: e.tensor_scalar(tk[:, 1, :], tk[:, 0, :], tks[:, 2, :], None, ALU.mult))
        P.op("vector", lambda e: e.reduce_max(tks[:, 3, :], tk[:, 1, :], AXX))
        P.op("vector", lambda e: e.tensor_scalar(tk[:, 2, :], tk[:, 1, :], tks[:, 3, :], None, ALU.is_equal))
        P.op("vector", lambda e: e.tensor_tensor(tk[:, 3, :], tk[:, 2, :], tk[:, 1, :], ALU.mult))
        P.op("vector", lambda e: e.tensor_tensor(tk[:, 4, :], tk[:, 1, :], tk[:, 3, :], ALU.subtract))
        P.op("vector", lambda e: e.reduce_max(tks[:, 4, :], tk[:, 4, :], AXX))
        P.op("vector", lambda e: e.tensor_scalar(tk[:, 5, :], tk[:, 1, :], tks[:, 4, :], None, ALU.is_ge))
        P.op("vector", lambda e: e.tensor_tensor(tks[:, 5, :], tks[:, 3, :], tks[:, 4, :], ALU.add))
        P.op("vector", lambda e: e.tensor_scalar(tks[:, 5, :], tks[:, 5, :], 1e-6, None, ALU.add))
        P.op("vector", lambda e: e.reciprocal(tks[:, 6, :], tks[:, 5, :]))
        P.op("vector", lambda e: e.tensor_scalar(tks[:, 6, :], tks[:, 6, :], 0.5, None, ALU.mult))
        P.op("vector", lambda e: e.tensor_tensor(tk[:, 6, :], tk[:, 1, :], tk[:, 5, :], ALU.mult))
        nv = P.op("vector", lambda e: e.tensor_scalar(comb[:, :], tk[:, 6, :], tks[:, 6, :], None, ALU.mult), "pv")
        P.guard_wait("vector", "C_sb")
        for ee in range(16):
            nv = P.op("vector", lambda e, ee=ee, ti=ti: e.tensor_scalar(
                C_sb[:, ee * 64:(ee + 1) * 64], rh_sb[:, ti, :], comb[:, ee:ee + 1],
                None, ALU.mult), "pv")
        P.wait("tensor", "pv", nv)
        for j in range(8):
            bank = pbT[j % 2]
            P.guard_wait("tensor", f"pbT{j % 2}y")
            nt = P.op("tensor", lambda e, ti=ti, j=j, bank=bank: e.transpose(
                bank[:, 0:128], C_sb[:, j * 128:(j + 1) * 128], idb_sb[:, :]), "pt")
            P.wait("vector", "pt", nt)
            nv = P.op("vector", lambda e, ti=ti, j=j, bank=bank: e.tensor_copy(
                ct_sb[:, j, ti * 128:(ti + 1) * 128], bank[:, 0:128]), "pv")
            P.guard_set(f"pbT{j % 2}y", "pv", nv)
        P.guard_set("C_sb", "pt", P.cnt["pt"])
    nct = P.cnt["pv"]
    P.wait("tensor", "pv", nct)

    for oc in range(NKC):
        par = oc % 2
        nw = phase_wstripe_T(P, bw, oc * 128, wf, wb, wt[:, par], idb_sb, pbT, f"s{par}")
        P.wait("tensor", f"s{par}v", nw)
        for tt in range(2):
            bank = pb[tt]
            P.guard_wait("tensor", f"pb{tt}")
            for kc in range(NKC):
                P.op("tensor", lambda e, kc=kc, tt=tt, par=par, bank=bank: e.matmul(
                    bank[:, :], wt[:, par, kc, :], xn[:, kc, tt * 512:(tt + 1) * 512],
                    start=(kc == 0), stop=False, skip_group_check=True), "pt")
            bpar = (oc * 2 + tt) % 2
            P.guard_wait("sync", f"br_sb{bpar}")
            nbr = P.dma("sync", br_sb[:, bpar],
                        br_dram[:, oc * 128:(oc + 1) * 128].rearrange("(a p) b -> p a b", p=128),
                        f"brd{bpar}")
            P.wait("tensor", f"brd{bpar}", nbr)
            nt = None
            for j in range(8):
                nt = P.op("tensor", lambda e, j=j, tt=tt, bpar=bpar, bank=bank: e.matmul(
                    bank[:, :], br_sb[:, bpar, j, :], ct_sb[:, j, tt * 512:(tt + 1) * 512],
                    start=False, stop=(j == 7), skip_group_check=True), "pt")
            P.guard_set(f"br_sb{bpar}", "pt", nt)
            P.guard_set(f"s{par}_wtv", "pt", nt)
            P.guard_wait("sync", f"xsl{tt}")
            nxs = P.dma("sync", xsl[:, tt, :], x[:, oc, tt * 512:(tt + 1) * 512], f"xs{tt}")
            P.wait("vector", "pt", nt)
            P.wait("vector", f"xs{tt}", nxs)
            P.guard_wait("vector", f"xo{tt}")
            nv = P.op("vector", lambda e, oc=oc, tt=tt, bank=bank: e.scalar_tensor_tensor(
                xo[:, tt, :], bank[:, :], bb_sb[:, oc:oc + 1], xsl[:, tt, :],
                ALU.add, ALU.add), "pv")
            P.guard_set(f"pb{tt}", "pv", nv)
            P.guard_set(f"xsl{tt}", "pv", nv)
            P.wait("sync", "pv", nv)
            nod = P.dma("sync", x4[:, oc, tt * 512:(tt + 1) * 512], xo[:, tt, :], "od")
            P.guard_set(f"xo{tt}", "od", nod)
    P.wait("sync", "od", P.cnt["od"])
    return P


def host_l7_inputs(c, inputs, x3_fm):
    return {
        "x_fm": x3_fm,
        "base_w": np.asarray(inputs["L1_ffn_base_w"], np.float32),
        "base_b": np.ascontiguousarray(
            np.asarray(inputs["L1_ffn_base_b"], np.float32).reshape(32, 128).T),
        "A_w": np.asarray(inputs["L1_ffn_A"], np.float32),
        "B_w": np.asarray(inputs["L1_ffn_B"], np.float32),
        "router_w": np.asarray(inputs["L1_ffn_router_w"], np.float32),
        "c_ident_bf": np.eye(128, dtype=np.float32).astype(BF),
        **common_host(),
    }



def build_l1_gather():
    P = Prog()
    cb = P.din("cb", [CB, 8], F32)
    idxg = P.din("idxg", [128, 11, 512], U32)
    idxu = P.din("idxu", [128, 11, 512], U32)
    idxd = P.din("idxd", [128, 11, 32, 16], U32)
    identb = P.din("c_ident_bf", [128, 128], BF16)
    wgT_o = P.dout("wgT_o", [4096, 1408], BF16)
    wuT_o = P.dout("wuT_o", [4096, 1408], BF16)
    wdT_o = P.dout("wdT_o", [1408, 4096], BF16)

    ig = P.sbuf("ig", [128, 11, 512], U32)
    iu = P.sbuf("iu", [128, 11, 512], U32)
    idn = P.sbuf("idn", [128, 11, 32, 16], U32)
    gout = P.sbuf("gout", [128, 2, 512, 8], BF16)
    wtb = P.sbuf("wtb", [128, 2, 32, 128], BF16)
    idb_sb = P.sbuf("idb_sb", [128, 128], BF16)

    pbT = [P.psum(f"pbT{i}", [128, 1024], BF16) for i in range(2)]

    P.sem_new("ld", "gt0", "gt1", "pt", "pv", "od0", "od1")
    P.dma("sync", ig[:, :, :], idxg, "ld")
    P.dma("sync", iu[:, :, :], idxu, "ld")
    P.dma("sync", idn[:, :, :, :], idxd, "ld")
    nld = P.dma("sync", idb_sb[:, :], identb, "ld")
    P.wait("gpsimd", "ld", nld)
    P.wait("tensor", "ld", nld)

    def emit_tile(rt_par, idx_ap_fn, ncalls, gsem):
        n = None
        for ccall in range(ncalls):
            n = P.inc(gsem, 16)
            s = P.sem[gsem]

            def call(e, ccall=ccall, rt_par=rt_par, idx_ap_fn=idx_ap_fn, s=s):
                return e.indirect_dma_start(
                    out=gout[:, rt_par, ccall, :], out_offset=None,
                    in_=cb[:, :],
                    in_offset=IOA(ap=idx_ap_fn(ccall), axis=0)).then_inc(s, 16)
            P.emit("gpsimd", call)
        return n

    def transpose_and_store(rt, par, out_dram, mode):
        nv = None
        for grp in range(4):
            bank = pbT[grp % 2]
            P.guard_wait("tensor", f"gpbT{grp % 2}")
            nt = None
            for i in range(8):
                kc = grp * 8 + i
                nt = P.op("tensor", lambda e, kc=kc, i=i, par=par, bank=bank: e.transpose(
                    bank[:, i * 128:(i + 1) * 128],
                    gout[:, par].rearrange("p a b -> p (a b)")[:, kc * 128:(kc + 1) * 128],
                    idb_sb[:, :]), "pt")
            P.wait("vector", "pt", nt)
            P.guard_wait("vector", f"wtb{par}")
            nv = P.op("vector", lambda e, grp=grp, par=par, bank=bank: e.tensor_copy(
                wtb[:, par, grp * 8:(grp + 1) * 8, :].rearrange("p a b -> p (a b)"),
                bank[:, :]), "pv")
            P.guard_set(f"gpbT{grp % 2}", "pv", nv)
        P.wait("sync", "pv", nv)
        if mode == "gu":
            nod = P.dma("sync",
                        out_dram[:, rt * 128:(rt + 1) * 128].rearrange("(a p) b -> p a b", p=128),
                        wtb[:, par], f"od{par}")
        else:
            nod = P.dma("sync",
                        out_dram[rt * 128:(rt + 1) * 128, :].rearrange("p (a b) -> p a b", b=128),
                        wtb[:, par], f"od{par}")
        P.guard_set(f"wtb{par}", f"od{par}", nod)
        return nod

    for mat, idx_sb, out_dram in (("g", ig, wgT_o), ("u", iu, wuT_o)):
        for rt in range(11):
            par = rt % 2
            P.guard_wait("gpsimd", f"gout{par}")
            ng = emit_tile(par, lambda ccall, rt=rt, idx_sb=idx_sb: idx_sb[:, rt, ccall:ccall + 1],
                           512, f"gt{par}")
            P.wait("tensor", f"gt{par}", ng)
            transpose_and_store(rt, par, out_dram, "gu")
            P.guard_set(f"gout{par}", "pt", P.cnt["pt"])
    for ic in range(11):
        par = ic % 2
        P.guard_wait("gpsimd", f"gout{par}")
        n = None
        for dt in range(32):
            for cc in range(16):
                n = P.inc(f"gt{par}", 16)
                s = P.sem[f"gt{par}"]

                def call(e, ic=ic, dt=dt, cc=cc, par=par, s=s):
                    return e.indirect_dma_start(
                        out=gout[:, par].rearrange("p a b -> p (a b)")[:, (dt * 16 + cc) * 8:(dt * 16 + cc + 1) * 8],
                        out_offset=None, in_=cb[:, :],
                        in_offset=IOA(ap=idn[:, ic, dt, cc:cc + 1], axis=0)).then_inc(s, 16)
                P.emit("gpsimd", call)
        P.wait("tensor", f"gt{par}", n)
        transpose_and_store(ic, par, wdT_o, "wd")
        P.guard_set(f"gout{par}", "pt", P.cnt["pt"])
    P.wait("sync", "od0", P.cnt["od0"])
    P.wait("sync", "od1", P.cnt["od1"])
    return P


def host_l1_inputs(c, inputs):
    cb = np.asarray(inputs["L0_ffn_codebook"], np.float32)
    ig = np.asarray(inputs["L0_ffn_idx_gate"]).astype(np.uint32)
    iu = np.asarray(inputs["L0_ffn_idx_up"]).astype(np.uint32)
    idn = np.asarray(inputs["L0_ffn_idx_down"]).astype(np.uint32)

    def pad_rows(a):
        sl = a[c * IC_PER_CORE:(c + 1) * IC_PER_CORE]
        p = np.zeros((1408, a.shape[1]), np.uint32)
        p[:1376] = sl
        return np.ascontiguousarray(p.reshape(11, 128, a.shape[1]).transpose(1, 0, 2))

    dsl = idn[:, c * 172:(c + 1) * 172]
    dp = np.zeros((4096, 176), np.uint32)
    dp[:, :172] = dsl
    d4 = dp.reshape(32, 128, 11, 16).transpose(1, 2, 0, 3)
    return {
        "cb": cb,
        "idxg": pad_rows(ig),
        "idxu": pad_rows(iu),
        "idxd": np.ascontiguousarray(d4),
        "c_ident_bf": np.eye(128, dtype=np.float32).astype(BF),
        **common_host(),
    }



_PROGS = {}


def _get_prog(name, builder):
    if name not in _PROGS:
        P = builder()
        _PROGS[name] = (P, P.build())
    return _PROGS[name]


def _run(name, builder, in_maps):
    from concourse.bass_utils import run_bass_kernel_spmd
    P, nc = _get_prog(name, builder)
    res = run_bass_kernel_spmd(nc, in_maps, core_ids=list(range(8)))
    return [res.results[i] for i in range(8)]


def kernel(**inputs):
    ins = {k: np.asarray(v) for k, v in inputs.items()}

    outs = _run("l1", build_l1_gather, [host_l1_inputs(c, ins) for c in range(8)])
    WgT = np.concatenate([np.asarray(o["wgT_o"]).view(BF)[:, :IC_PER_CORE] for o in outs], axis=1)
    WuT = np.concatenate([np.asarray(o["wuT_o"]).view(BF)[:, :IC_PER_CORE] for o in outs], axis=1)
    WdT = np.concatenate([np.asarray(o["wdT_o"]).view(BF)[:IC_PER_CORE, :] for o in outs], axis=0)
    WgT = np.ascontiguousarray(WgT)
    WuT = np.ascontiguousarray(WuT)
    WdT = np.ascontiguousarray(WdT)

    x0 = np.asarray(ins["L0_attn_hidden_states"], np.float32)[0]
    x_fms = {}
    for c in range(8):
        x_fms[c] = fm(x0[own_token_ids(c)]).reshape(128, NKC, 1024)
    outs = _run("l2", build_l2_qkv0, [host_l2_inputs(c, ins) for c in range(8)])
    K_full = np.zeros((8, 128, SEQ), BF)
    V_full = np.zeros((8, SEQ, 128), BF)
    q_sh = {}
    for c in range(8):
        ids = own_token_ids(c)
        q_sh[c] = np.asarray(outs[c]["q_fm"]).view(BF)
        kf = np.asarray(outs[c]["k_fm"]).view(BF)
        vf = np.asarray(outs[c]["v_tm"]).view(BF)
        for g in range(8):
            K_full[g][:, ids] = kf[g]
            V_full[g][ids, :] = vf[g]

    ims = [host_l3_inputs(c, ins, x_fms[c], q_sh[c], K_full, V_full) for c in range(8)]
    outs = _run("l3", build_l3_attn0, ims)
    x1 = {c: np.ascontiguousarray(np.asarray(outs[c]["x1_fm"], np.float32).reshape(128, NKC, 1024))
          for c in range(8)}

    ims = [host_l4_inputs(c, x1[c], WgT, WuT, WdT) for c in range(8)]
    outs = _run("l4", build_l4_ffn0, ims)
    x2 = {c: np.ascontiguousarray(np.asarray(outs[c]["x2_fm"], np.float32).reshape(128, NKC, 1024))
          for c in range(8)}

    ims = [host_l5_inputs(c, ins, x2[c]) for c in range(8)]
    outs = _run("l5", build_l5_qkv1, ims)
    K1 = np.zeros((128, SEQ), BF)
    V1 = np.zeros((SEQ, 128), BF)
    q1_sh = {}
    for c in range(8):
        ids = own_token_ids(c)
        q1_sh[c] = np.asarray(outs[c]["q1_fm"]).view(BF)
        K1[:, ids] = np.asarray(outs[c]["k1_fm"]).view(BF)
        V1[ids, :] = np.asarray(outs[c]["v1_tm"]).view(BF)

    ims = [host_l6_inputs(c, ins, x2[c], q1_sh[c], np.ascontiguousarray(K1),
                          np.ascontiguousarray(V1)) for c in range(8)]
    outs = _run("l6", build_l6_attn1, ims)
    x3 = {c: np.ascontiguousarray(np.asarray(outs[c]["x3_fm"], np.float32).reshape(128, NKC, 1024))
          for c in range(8)}

    ims = [host_l7_inputs(c, ins, x3[c]) for c in range(8)]
    outs = _run("l7", build_l7_ffn1, ims)
    out = np.zeros((SEQ, D), np.float32)
    for c in range(8):
        ids = own_token_ids(c)
        x4 = np.asarray(outs[c]["x4_fm"], np.float32).reshape(128, NKC, 1024)
        out[ids] = un_fm(x4)
    return out[None]


# version 04 (final)
# speedup vs baseline: 1.0355x; 1.0355x over previous
N_CORES = 8
BLOCK = 64
N_BLOCKS = 128
SEQ = 8192
D = 4096
NKC = D // 128
OWN_TOK = 1024
LOCAL_BLOCKS = 16
VERT = 8
NH = 32
NKV = 8
HD = 128
I_FFN = 11008
NIC = I_FFN // 128
IC_PER_CORE = 1376
CS = 8
CB = 65536


def own_blocks(c):
    return [c + 8 * j for j in range(16)]


def own_token_ids(c):
    import numpy as np
    ids = []
    for b in own_blocks(c):
        ids.extend(range(b * BLOCK, (b + 1) * BLOCK))
    return np.array(ids)


def l0_allowed(qb, b):
    return b <= qb and (qb - b < LOCAL_BLOCKS or (b + 1) % VERT == 0)


def l0_plan(qb):
    lo = max(0, qb - (LOCAL_BLOCKS - 1))
    ops = []
    t0, t1 = lo // 2, qb // 2
    for t in range(t0, t1 + 1):
        bl, bh = 2 * t, 2 * t + 1
        ok_l = lo <= bl <= qb
        ok_h = lo <= bh <= qb
        if ok_l and ok_h:
            ml = "causal" if bl == qb else None
            mh = "causal" if bh == qb else None
            ops.append(("tile", t, ml, mh))
        elif ok_l:
            ops.append(("block", bl, "causal" if bl == qb else None))
        elif ok_h:
            ops.append(("block", bh, "causal" if bh == qb else None))
    b = 7
    while b < lo:
        ops.append(("block", b, None))
        b += 8
    return ops


def l1_plan(c, qh_chunk_tokens=512):
    plans = []
    blocks = own_blocks(c)
    for t in range(N_BLOCKS // 2):
        bl, bh = 2 * t, 2 * t + 1
        j0 = None
        for j, b in enumerate(blocks):
            if b >= bl:
                j0 = j
                break
        if j0 is None:
            continue
        masks = []
        b0 = blocks[j0]
        if b0 == bl:
            masks.append((0, j0, "causal"))
            masks.append((1, j0, "zero"))
        elif b0 == bh:
            masks.append((1, j0, "causal"))
        elif b0 < bh:
            raise AssertionError
        plans.append((t, j0, masks))
    return plans


def causal_mask64():
    import numpy as np
    i = np.arange(BLOCK)
    return (i[:, None] <= i[None, :]).astype(np.float32)


def ffn_shard_rows(c):
    return c * IC_PER_CORE, (c + 1) * IC_PER_CORE


def rope_tables():
    import numpy as np
    inv = 1.0 / (1000000.0 ** (np.arange(0, HD, 2, dtype=np.float64) / HD))
    t = np.arange(SEQ, dtype=np.float64)
    freqs = np.outer(t, inv)
    emb = np.concatenate([freqs, freqs], axis=1)
    return np.cos(emb).astype(np.float32), np.sin(emb).astype(np.float32)



def l0_strided_slots(j):
    return max(0, j - 1)


def l0_strided_pairs(j):
    return (l0_strided_slots(j) + 1) // 2


def build_l0_masks(c):
    import numpy as np
    cm = causal_mask64()
    Mwin = np.zeros((2, 128, 512), np.float32)
    for j in range(2):
        qb = c + 8 * j
        for w in range(8):
            for half in range(2):
                b = 2 * w + half
                rows = slice(half * 64, (half + 1) * 64)
                cols = slice(w * 64, (w + 1) * 64)
                if b < qb:
                    Mwin[j][rows, cols] = 1.0
                elif b == qb:
                    Mwin[j][rows, cols] = cm
    Mstr = np.zeros((16, 128, 7 * 64), np.float32)
    for j in range(16):
        qb = c + 8 * j
        for sp in range(7):
            for half in range(2):
                m = 2 * sp + half
                rows = slice(half * 64, (half + 1) * 64)
                cols = slice(sp * 64, (sp + 1) * 64)
                if 8 * m + 7 < qb - 15:
                    Mstr[j][rows, cols] = 1.0
    return Mwin, Mstr



def l1_j0u(t):
    return max(0, -(-(2 * t - 7) // 8))


def build_l1_masks(c):
    import numpy as np
    cm = causal_mask64()
    M = np.zeros((64, 128, 128), np.float32)
    for t in range(64):
        j0 = l1_j0u(t)
        for dj in range(2):
            j = j0 + dj
            if j > 15:
                continue
            qb = c + 8 * j
            cols = slice(dj * 64, (dj + 1) * 64)
            for half in range(2):
                b = 2 * t + half
                rows = slice(half * 64, (half + 1) * 64)
                if b < qb:
                    M[t][rows, cols] = 1.0
                elif b == qb:
                    M[t][rows, cols] = cm
    return M


"""Bass emitters for the mk_seq_preln_d2 kernel.

Phases append (engine, closure) ops to per-engine lists; Block sections replay
them. Semaphore counts tracked at append time. All launches SPMD-uniform.
"""
import numpy as np
from collections import defaultdict
from contextlib import ExitStack

import concourse.bass as bass
import concourse.mybir as mybir



F32 = mybir.dt.float32
BF16 = mybir.dt.bfloat16
I32 = mybir.dt.int32
U32 = mybir.dt.uint32
AF = mybir.ActivationFunctionType
ALU = mybir.AluOpType
IOA = bass.IndirectOffsetOnAxis
ds = bass.ds

ENGINES = ("tensor", "vector", "scalar", "gpsimd", "sync")


class Prog:
    def __init__(self, race=True):
        self.nc = bass.Bass("TRN2", target_bir_lowering=False, debug=False,
                            detect_race_conditions=race)
        self.stack = ExitStack()
        self.ops = {e: [] for e in ENGINES}
        self.sem = {}
        self.cnt = defaultdict(int)
        self.sb = {}
        self.dram = {}
        self.guards = {}

    def sem_new(self, *names):
        for n in names:
            if n not in self.sem:
                self.sem[n] = self.stack.enter_context(self.nc.semaphore(n))

    def sbuf(self, name, shape, dtype):
        t = self.stack.enter_context(self.nc.sbuf_tensor(name, shape, dtype))
        self.sb[name] = t
        return t

    def psum(self, name, shape, dtype=F32):
        return self.stack.enter_context(self.nc.psum_tensor(name, shape, dtype))

    def din(self, name, shape, dtype):
        ap = self.nc.dram_tensor(name, shape, dtype, kind="ExternalInput").ap()
        self.dram[name] = ap
        return ap

    def dout(self, name, shape, dtype):
        ap = self.nc.dram_tensor(name, shape, dtype, kind="ExternalOutput").ap()
        self.dram[name] = ap
        return ap

    def emit(self, eng, fn):
        self.ops[eng].append(fn)

    def inc(self, sem_name, by):
        self.cnt[sem_name] += by
        return self.cnt[sem_name]

    def dma(self, q, out, in_, sem):
        n = self.inc(sem, 16)
        s = self.sem[sem]
        self.emit(q, lambda e, out=out, in_=in_: e.dma_start(out=out, in_=in_).then_inc(s, 16))
        return n

    def wait(self, eng, sem, n):
        if n is None or n <= 0:
            return
        s = self.sem[sem]
        self.emit(eng, lambda e, n=n: e.wait_ge(s, n))

    def op(self, eng, f, sem=None):
        if sem is None:
            self.emit(eng, f)
            return None
        n = self.inc(sem, 1)
        s = self.sem[sem]
        self.emit(eng, lambda e, f=f: f(e).then_inc(s, 1))
        return n

    def guard_set(self, key, sem, n):
        self.guards[key] = (sem, n)

    def guard_wait(self, eng, key):
        if key in self.guards:
            s, n = self.guards[key]
            self.wait(eng, s, n)

    def build(self):
        with self.nc.Block() as block:
            for e in ENGINES:
                ops = self.ops[e]
                if not ops:
                    continue

                def body(eng, ops=ops):
                    for f in ops:
                        f(eng)
                getattr(block, e)(body)
        self.stack.close()
        return self.nc


def alloc_common(P):
    P.sbuf("ones_bf", [128, 1], BF16)
    P.sbuf("ones_f", [128, 1], F32)
    P.sbuf("ones1_f", [1, 128], F32)


def load_common(P, sem="c0"):
    P.sem_new(sem)
    P.dma("sync", P.sb["ones_bf"][:, :], P.din("c_ones_bf", [128, 1], BF16), sem)
    P.dma("sync", P.sb["ones_f"][:, :], P.din("c_ones_f", [128, 1], F32), sem)
    n = P.dma("sync", P.sb["ones1_f"][:, :], P.din("c_ones1_f", [1, 128], F32), sem)
    return n


def common_host():
    import ml_dtypes
    return {
        "c_ones_bf": np.ones((128, 1), np.float32).astype(ml_dtypes.bfloat16),
        "c_ones_f": np.ones((128, 1), np.float32),
        "c_ones1_f": np.ones((1, 128), np.float32),
    }


def alloc_ln_scratch(P, TT):
    P.sbuf("ln_xq", [128, NKC, TT], BF16)
    P.sbuf("ln_st", [1, 8, TT], F32)
    P.sbuf("ln_tmp", [128, 2, TT], F32)


def phase_ln(P, x_ap, TT, xn_view, pfx, ps1, ps2, psb, psm):
    P.sem_new(pfx + "a", pfx + "v", pfx + "t")
    xq, st, tmp = P.sb["ln_xq"], P.sb["ln_st"], P.sb["ln_tmp"]
    ones_bf, ones_f, ones1_f = P.sb["ones_bf"], P.sb["ones_f"], P.sb["ones1_f"]

    P.guard_wait("scalar", "ln_xq")
    na = P.op("scalar", lambda e: e.activation(xq[:, :, :TT], x_ap, AF.Square), pfx + "a")
    P.wait("tensor", pfx + "a", na)
    nt = None
    for kc in range(NKC):
        s0, s1_ = (kc == 0), (kc == NKC - 1)
        P.op("tensor", lambda e, kc=kc, s0=s0, s1_=s1_: e.matmul(
            ps1, ones_f[:, :], x_ap[:, kc, :], start=s0, stop=s1_))
        nt = P.op("tensor", lambda e, kc=kc, s0=s0, s1_=s1_: e.matmul(
            ps2, ones_bf[:, :], xq[:, kc, :TT], start=s0, stop=s1_), pfx + "t")
    P.guard_set("ln_xq", pfx + "t", nt)
    P.wait("vector", pfx + "t", nt)
    rD = 1.0 / D
    P.op("vector", lambda e: e.tensor_scalar(st[:, 0, :TT], ps1, rD, None, ALU.mult))
    P.op("vector", lambda e: e.tensor_scalar(st[:, 1, :TT], ps2, rD, None, ALU.mult))
    P.op("vector", lambda e: e.tensor_tensor(st[:, 2, :TT], st[:, 0, :TT], st[:, 0, :TT], ALU.mult))
    P.op("vector", lambda e: e.tensor_tensor(st[:, 3, :TT], st[:, 1, :TT], st[:, 2, :TT], ALU.subtract))
    nv = P.op("vector", lambda e: e.tensor_scalar(st[:, 3, :TT], st[:, 3, :TT], 1e-5, None, ALU.add), pfx + "v")
    P.wait("scalar", pfx + "v", nv)
    na = P.op("scalar", lambda e: e.activation(st[:, 4, :TT], st[:, 3, :TT], AF.Sqrt), pfx + "a")
    P.wait("vector", pfx + "a", na)
    P.op("vector", lambda e: e.reciprocal(st[:, 5, :TT], st[:, 4, :TT]))
    nv = P.op("vector", lambda e: e.tensor_tensor(st[:, 6, :TT], st[:, 0, :TT], st[:, 5, :TT], ALU.mult), pfx + "v")
    P.wait("tensor", pfx + "v", nv)
    P.op("tensor", lambda e: e.matmul(psb, ones1_f[:, :], st[:, 5, :TT], start=True, stop=True))
    nt = P.op("tensor", lambda e: e.matmul(psm, ones1_f[:, :], st[:, 6, :TT], start=True, stop=True), pfx + "t")
    P.wait("vector", pfx + "t", nt)
    nv = None
    for kc in range(NKC):
        b = kc % 2
        P.op("vector", lambda e, kc=kc, b=b: e.tensor_tensor(tmp[:, b, :TT], x_ap[:, kc, :], psb, ALU.mult))
        nv = P.op("vector", lambda e, kc=kc, b=b: e.tensor_tensor(
            xn_view[:, kc, :], tmp[:, b, :TT], psm, ALU.subtract), pfx + "v")
    return nv


def phase_wstripe_T(P, w_dram, row0, wf, wb, wt_view, ident_bf, pbT, pfx, ncin=NKC):
    P.sem_new(pfx + "d", pfx + "a", pfx + "t", pfx + "v")
    half = ncin * 128 // 2
    for hh in range(2):
        P.guard_wait("sync", f"{pfx}_wf{hh}")
        nd = P.dma("sync", wf[:, hh * half:(hh + 1) * half],
                   w_dram[row0:row0 + 128, hh * half:(hh + 1) * half], pfx + "d")
        P.wait("scalar", pfx + "d", nd)
        P.guard_wait("scalar", pfx + "_wb")
        na = P.op("scalar", lambda e, hh=hh: e.activation(
            wb[:, hh * half:(hh + 1) * half], wf[:, hh * half:(hh + 1) * half],
            AF.Copy, bias=0.0), pfx + "a")
        P.guard_set(f"{pfx}_wf{hh}", pfx + "a", na)
    P.wait("tensor", pfx + "a", na)
    nv = None
    for grp in range(ncin // 8):
        bank = pbT[grp % 2]
        P.guard_wait("tensor", f"{pfx}_pbT{grp % 2}")
        nt = None
        for i in range(8):
            kc = grp * 8 + i
            nt = P.op("tensor", lambda e, kc=kc, i=i, bank=bank: e.transpose(
                bank[:, i * 128:(i + 1) * 128], wb[:, kc * 128:(kc + 1) * 128],
                ident_bf[:, :]), pfx + "t")
        P.wait("vector", pfx + "t", nt)
        P.guard_wait("vector", pfx + "_wtv")
        nv = P.op("vector", lambda e, grp=grp, bank=bank: e.tensor_copy(
            wt_view[:, grp * 8:(grp + 1) * 8, :].rearrange("p a b -> p (a b)"),
            bank[:, :]), pfx + "v")
        P.guard_set(f"{pfx}_pbT{grp % 2}", pfx + "v", nv)
    P.guard_set(pfx + "_wb", pfx + "t", P.cnt[pfx + "t"])
    return nv


"""Launch builders for the unfused mk kernel."""
import numpy as np
import ml_dtypes

import concourse.mybir as mybir

AXX = mybir.AxisListType.X




BF = ml_dtypes.bfloat16


def fm(x_tok_major):
    Tn, D = x_tok_major.shape
    return np.ascontiguousarray(
        x_tok_major.reshape(Tn, D // 128, 128).transpose(2, 1, 0))


def un_fm(x_fm):
    return np.ascontiguousarray(x_fm.transpose(2, 1, 0).reshape(x_fm.shape[2], -1))



def build_l2_qkv0():
    P = Prog()
    x = P.din("x_fm", [128, NKC, 1024], F32)
    w = P.din("w_qkv", [6144, 4096], F32)
    bq = P.din("bqkv", [128, 48], F32)
    cos = P.din("cos_own", [128, 1024], F32)
    sin = P.din("sin_own", [128, 1024], F32)
    rotT = P.din("c_rot_T", [128, 128], F32)
    identb = P.din("c_ident_bf", [128, 128], BF16)
    q_out = P.dout("q_fm", [32, 128, 1024], BF16)
    k_out = P.dout("k_fm", [8, 128, 1024], BF16)
    v_out = P.dout("v_tm", [8, 1024, 128], BF16)

    alloc_common(P)
    alloc_ln_scratch(P, 256)
    x_t = P.sbuf("x_t", [128, NKC, 256], F32)
    xn = P.sbuf("xn", [128, NKC, 1024], BF16)
    bq_sb = P.sbuf("bq_sb", [128, 48], F32)
    cos_sb = P.sbuf("cos_sb", [128, 1024], F32)
    sin_sb = P.sbuf("sin_sb", [128, 1024], F32)
    rot_sb = P.sbuf("rot_sb", [128, 128], F32)
    idb_sb = P.sbuf("idb_sb", [128, 128], BF16)
    wf = P.sbuf("wf", [128, 4096], F32)
    wb = P.sbuf("wb", [128, 4096], BF16)
    af = wf[0:64, :]
    ab = wb[0:64, :]
    rf = wf[0:16, :]
    rb_ = wb[0:16, :]
    wt = P.sbuf("wt", [128, 2, NKC, 128], BF16)
    praw = P.sbuf("praw", [128, 2, 512], F32)
    rope1 = P.sbuf("rope1", [128, 2, 512], F32)
    obuf = P.sbuf("obuf", [128, 2, 512], BF16)
    vtm = P.sbuf("vtm", [128, 2, 4, 128], BF16)

    pb = [P.psum(f"pb{i}", [128, 512], F32) for i in range(4)]
    pbT = [P.psum(f"pbT{i}", [128, 1024], BF16) for i in range(2)]
    pbR = P.psum("pbR", [128, 512], F32)
    pbV = P.psum("pbV", [128, 512], BF16)

    P.sem_new("c0", "ld", "wd", "wa", "wt", "wv", "pt", "pv", "od", "xd")
    n0 = load_common(P)
    P.dma("sync", bq_sb[:, :], bq, "ld")
    P.dma("sync", cos_sb[:, :], cos, "ld")
    P.dma("sync", sin_sb[:, :], sin, "ld")
    P.dma("sync", rot_sb[:, :], rotT, "ld")
    nld = P.dma("sync", idb_sb[:, :], identb, "ld")
    for e in ("scalar", "vector", "tensor"):
        P.wait(e, "ld", nld)
        P.wait(e, "c0", n0)

    for st_ in range(4):
        P.guard_wait("sync", "x_t")
        nxd = P.dma("sync", x_t[:, :, :], x[:, :, st_ * 256:(st_ + 1) * 256], "xd")
        P.wait("scalar", "xd", nxd)
        P.wait("tensor", "xd", nxd)
        P.wait("vector", "xd", nxd)
        nv = phase_ln(P, x_t[:, :, :], 256, xn[:, :, st_ * 256:(st_ + 1) * 256],
                      f"ln{st_}", pb[0][0:1, :256], pb[1][0:1, :256],
                      pb[2][:, :256], pb[3][:, :256])
        P.guard_set("x_t", f"ln{st_}v", nv)
    nxn = nv
    P.wait("tensor", "ln3v", nxn)

    for oc in range(48):
        par = oc % 2
        nw = phase_wstripe_T(P, w, oc * 128, wf, wb, wt[:, par], idb_sb, pbT, f"s{par}")
        P.wait("tensor", f"s{par}v", nw)
        g, slot = oc // 6, oc % 6
        is_q, is_k = slot < 4, slot == 4
        for tt in range(2):
            bank = pb[tt]
            P.guard_wait("tensor", f"pb{tt}")
            nt = None
            for kc in range(NKC):
                nt = P.op("tensor", lambda e, kc=kc, tt=tt, bank=bank, par=par: e.matmul(
                    bank[:, :], wt[:, par, kc, :], xn[:, kc, tt * 512:(tt + 1) * 512],
                    start=(kc == 0), stop=(kc == NKC - 1)), "pt")
            P.guard_set(f"s{par}_wtv", "pt", nt)
            P.wait("vector", "pt", nt)
            P.guard_wait("vector", f"praw{tt}")
            nv = P.op("vector", lambda e, tt=tt, oc=oc, bank=bank: e.tensor_scalar(
                praw[:, tt, :], bank[:, :], bq_sb[:, oc:oc + 1], None, ALU.add), "pv")
            P.guard_set(f"pb{tt}", "pv", nv)
            if is_q or is_k:
                P.wait("tensor", "pv", nv)
                P.guard_wait("tensor", "pbR")
                ntr = P.op("tensor", lambda e, tt=tt: e.matmul(
                    pbR[:, :], rot_sb[:, :], praw[:, tt, :], start=True, stop=True), "pt")
                P.wait("vector", "pt", ntr)
                cs = cos_sb[:, tt * 512:(tt + 1) * 512]
                sn = sin_sb[:, tt * 512:(tt + 1) * 512]
                P.op("vector", lambda e, tt=tt, cs=cs: e.tensor_tensor(
                    rope1[:, tt, :], praw[:, tt, :], cs, ALU.mult))
                nv2 = P.op("vector", lambda e, tt=tt, sn=sn: e.tensor_tensor(
                    praw[:, tt, :], pbR[:, :], sn, ALU.mult), "pv")
                P.guard_set("pbR", "pv", nv2)
                P.guard_set(f"praw{tt}", "pt", ntr)
                P.guard_wait("vector", f"obuf{tt}")
                nv3 = P.op("vector", lambda e, tt=tt: e.tensor_tensor(
                    obuf[:, tt, :], rope1[:, tt, :], praw[:, tt, :], ALU.add), "pv")
                P.wait("sync", "pv", nv3)
                if is_q:
                    nod = P.dma("sync", q_out[g * 4 + slot, :, tt * 512:(tt + 1) * 512],
                                obuf[:, tt, :], "od")
                else:
                    nod = P.dma("sync", k_out[g, :, tt * 512:(tt + 1) * 512],
                                obuf[:, tt, :], "od")
                P.guard_set(f"obuf{tt}", "od", nod)
            elif slot == 5:
                P.guard_wait("vector", f"obuf{tt}")
                nv2 = P.op("vector", lambda e, tt=tt: e.tensor_copy(
                    obuf[:, tt, :], praw[:, tt, :]), "pv")
                P.wait("tensor", "pv", nv2)
                P.guard_wait("tensor", "pbV")
                ntv = None
                for i in range(4):
                    ntv = P.op("tensor", lambda e, tt=tt, i=i: e.transpose(
                        pbV[:, i * 128:(i + 1) * 128],
                        obuf[:, tt, i * 128:(i + 1) * 128], idb_sb[:, :]), "pt")
                P.guard_set(f"obuf{tt}", "pt", ntv)
                P.wait("vector", "pt", ntv)
                P.guard_wait("vector", f"vtm{tt}")
                nv3 = P.op("vector", lambda e, tt=tt: e.tensor_copy(
                    vtm[:, tt, :, :].rearrange("p a b -> p (a b)"), pbV[:, :]), "pv")
                P.guard_set("pbV", "pv", nv3)
                P.wait("sync", "pv", nv3)
                nod = P.dma("sync",
                            v_out[g, tt * 512:(tt + 1) * 512, :].rearrange("(a p) b -> p a b", p=128),
                            vtm[:, tt, :, :], "od")
                P.guard_set(f"vtm{tt}", "od", nod)
    P.wait("sync", "od", P.cnt["od"])
    return P


def host_l2_inputs(c, inputs):
    ids = own_token_ids(c)
    x = np.asarray(inputs["L0_attn_hidden_states"], np.float32)[0]
    cos, sin = rope_tables()
    rot = np.zeros((128, 128), np.float32)
    for d in range(64):
        rot[d, 64 + d] = -1.0
        rot[64 + d, d] = 1.0
    return {
        "x_fm": fm(x[ids]).reshape(128, NKC, 1024),
        "w_qkv": np.asarray(inputs["L0_attn_w_qkv"], np.float32),
        "bqkv": np.ascontiguousarray(
            np.asarray(inputs["L0_attn_b_qkv"], np.float32).reshape(48, 128).T),
        "cos_own": np.ascontiguousarray(cos[ids].T),
        "sin_own": np.ascontiguousarray(sin[ids].T),
        "c_rot_T": np.ascontiguousarray(rot.T),
        "c_ident_bf": np.eye(128, dtype=np.float32).astype(BF),
        **common_host(),
    }



def build_l3_attn0():
    P = Prog()
    x = P.din("x_fm", [128, NKC, 1024], F32)
    q_in = P.din("q_fm", [32, 128, 1024], BF16)
    K = P.din("K_full", [8, 128, 8192], BF16)
    V = P.din("V_full", [8, 8192, 128], BF16)
    Ksh = P.din("K_sh", [8, 128, 8192], BF16)
    Vsh = P.din("V_sh", [8, 8192, 128], BF16)
    wd = P.din("w_dense", [4096, 4096], F32)
    bd = P.din("bdense", [128, 32], F32)
    mwin = P.din("m_win", [2, 128, 512], BF16)
    mstr = P.din("m_str", [16, 128, 448], BF16)
    cmask = P.din("c_causal", [128, 64], BF16)
    identb = P.din("c_ident_bf", [128, 128], BF16)
    x1 = P.dout("x1_fm", [128, NKC, 1024], F32)

    alloc_common(P)
    ctx = P.sbuf("ctx", [128, NH, 1024], BF16)
    q_sb = P.sbuf("q_sb", [128, 4, 1024], BF16)
    kwin = P.sbuf("kwin", [128, 1024], BF16)
    vwin = P.sbuf("vwin", [128, 8, 128], BF16)
    kstr = P.sbuf("kstr", [128, 14, 64], BF16)
    vstr = P.sbuf("vstr", [128, 7, 128], BF16)
    pw = P.sbuf("pw", [128, 512], BF16)
    pstr = P.sbuf("pstr", [128, 448], BF16)
    mw_sb = P.sbuf("mw_sb", [128, 2, 512], BF16)
    ms_sb = P.sbuf("ms_sb", [128, 16, 448], BF16)
    cm_sb = P.sbuf("cm_sb", [128, 64], BF16)
    idb_sb = P.sbuf("idb_sb", [128, 128], BF16)
    den = P.sbuf("den", [1, 4, 64], F32)
    den2 = P.sbuf("den2", [1, 64], F32)
    denr = P.sbuf("denr", [1, 256], F32)
    rb = P.sbuf("rb", [128, 256], F32)
    bd_sb = P.sbuf("bd_sb", [128, 32], F32)
    xsl = P.sbuf("xsl", [128, 2, 512], F32)
    xo = P.sbuf("xo", [128, 2, 512], F32)
    wf = P.sbuf("wf", [128, 4096], F32)
    wb = P.sbuf("wb", [128, 4096], BF16)
    af = wf[0:64, :]
    ab = wb[0:64, :]
    rf = wf[0:16, :]
    rb_ = wb[0:16, :]
    wt = P.sbuf("wt", [128, 2, NKC, 128], BF16)

    psS = P.psum("psS", [128, 512], F32)
    psR = P.psum("psR", [128, 512], F32)
    psC = P.psum("psC", [128, 512], F32)
    psD1 = P.psum("psD1", [1, 512], F32)
    psD2 = P.psum("psD2", [1, 512], F32)
    psB = P.psum("psB", [128, 512], F32)
    pbT = [P.psum(f"pbT{i}", [128, 1024], BF16) for i in range(2)]

    P.sem_new("c0", "ld", "kq", "ks", "kw", "pt", "pa", "pv", "od", "xd0", "xd1")
    n0 = load_common(P)
    P.dma("sync", mw_sb[:, :, :], mwin.rearrange("a p b -> p a b"), "ld")
    P.dma("sync", ms_sb[:, :, :], mstr.rearrange("a p b -> p a b"), "ld")
    P.dma("sync", cm_sb[:, :], cmask, "ld")
    P.dma("sync", bd_sb[:, :], bd, "ld")
    nld = P.dma("sync", idb_sb[:, :], identb, "ld")
    for e in ("scalar", "vector", "tensor"):
        P.wait(e, "ld", nld)
        P.wait(e, "c0", n0)

    SM = 1.0 / HD


    for g in range(8):
        P.guard_wait("sync", "q_sb")
        nkq = P.dma("sync", q_sb[:, :, :],
                    q_in.rearrange("a p b -> p a b")[:, 4 * g:4 * g + 4, :], "kq")
        P.guard_wait("sync", "kstr")
        P.dma("sync", kstr[:, :, :],
              K[g].rearrange("p (m i) -> p m i", i=64)[:, 7:119:8, :], "ks")
        P.guard_wait("sync", "vstr")
        P.dma("sync", vstr[0:64, :, :],
              V[g].rearrange("(m i) d -> i m d", i=64)[:, 7:112:16, :], "ks")
        nks = P.dma("sync", vstr[64:128, :, :],
                    V[g].rearrange("(m i) d -> i m d", i=64)[:, 15:120:16, :], "ks")
        P.wait("tensor", "kq", nkq)
        P.wait("tensor", "ks", nks)
        for j in range(16):
            npairs = l0_strided_pairs(j)
            P.guard_wait("sync", "kwin")
            if j < 2:
                P.dma("sync", kwin[:, :], K[g, :, 0:1024], "kw")
                nwd = P.dma("sync", vwin[:, :, :],
                            V[g, 0:1024, :].rearrange("(t p) d -> p t d", p=128), "kw")
            else:
                off = (8 * j - 15) * 64
                P.dma("sync", kwin[:, :], Ksh[g, :, off:off + 1024], "kw")
                nwd = P.dma("sync", vwin[:, :, :],
                            Vsh[g, off:off + 1024, :].rearrange("(t p) d -> p t d", p=128), "kw")
            P.wait("tensor", "kw", nwd)
            for hh in range(4):
                qa = q_sb[:, hh, j * 64:(j + 1) * 64]
                P.guard_wait("tensor", "psS")
                nt = None
                for w in range(8):
                    nt = P.op("tensor", lambda e, w=w, qa=qa: e.matmul(
                        psS[:, w * 64:(w + 1) * 64], kwin[:, w * 128:(w + 1) * 128],
                        qa, start=True, stop=True, skip_group_check=True), "pt")
                P.wait("scalar", "pt", nt)
                P.guard_wait("scalar", "pw")
                na = P.op("scalar", lambda e: e.activation(
                    pw[:, :], psS[:, :], AF.Exp, scale=SM), "pa")
                P.guard_set("psS", "pa", na)
                P.wait("vector", "pa", na)
                if j < 2:
                    nv = P.op("vector", lambda e, j=j: e.tensor_tensor(
                        pw[:, :], pw[:, :], mw_sb[:, j, :], ALU.mult), "pv")
                else:
                    nv = P.op("vector", lambda e: e.tensor_tensor(
                        pw[:, 448:512], pw[:, 448:512], cm_sb[:, :], ALU.mult), "pv")
                if npairs > 0:
                    P.guard_wait("tensor", "psR")
                    nt2 = None
                    for sp in range(npairs):
                        nt2 = P.op("tensor", lambda e, sp=sp, qa=qa: e.matmul(
                            psR[:, sp * 64:(sp + 1) * 64],
                            kstr[:, 2 * sp:2 * sp + 2, :].rearrange("p a b -> p (a b)"),
                            qa, start=True, stop=True, skip_group_check=True), "pt")
                    P.wait("scalar", "pt", nt2)
                    P.guard_wait("scalar", "pstr")
                    na2 = P.op("scalar", lambda e, npairs=npairs: e.activation(
                        pstr[:, :npairs * 64], psR[:, :npairs * 64], AF.Exp, scale=SM), "pa")
                    P.guard_set("psR", "pa", na2)
                    P.wait("vector", "pa", na2)
                    nv2 = P.op("vector", lambda e, j=j, npairs=npairs: e.tensor_tensor(
                        pstr[:, :npairs * 64], pstr[:, :npairs * 64],
                        ms_sb[:, j, :npairs * 64], ALU.mult), "pv")
                else:
                    nv2 = nv
                P.wait("tensor", "pv", max(nv, nv2))
                if hh == 0:
                    P.guard_wait("tensor", "psC")
                ntp = None
                npv = 8 + npairs
                for w in range(8):
                    ntp = P.op("tensor", lambda e, w=w, hh=hh, npv=npv: e.matmul(
                        psC[:, hh * 64:(hh + 1) * 64], vwin[:, w, :],
                        pw[:, w * 64:(w + 1) * 64],
                        start=(w == 0), stop=(npairs == 0 and w == 7),
                        skip_group_check=True), "pt")
                for sp in range(npairs):
                    ntp = P.op("tensor", lambda e, sp=sp, hh=hh: e.matmul(
                        psC[:, hh * 64:(hh + 1) * 64], vstr[:, sp, :],
                        pstr[:, sp * 64:(sp + 1) * 64],
                        start=False, stop=(sp == npairs - 1),
                        skip_group_check=True), "pt")
                P.guard_wait("tensor", "psD1")
                nt3 = P.op("tensor", lambda e: e.matmul(
                    psD1[:, :], P.sb["ones_bf"][:, :], pw[:, :], start=True, stop=True), "pt")
                if npairs > 0:
                    P.guard_wait("tensor", "psD2")
                    nt4 = P.op("tensor", lambda e, npairs=npairs: e.matmul(
                        psD2[:, :npairs * 64], P.sb["ones_bf"][:, :], pstr[:, :npairs * 64],
                        start=True, stop=True), "pt")
                else:
                    nt4 = nt3
                P.guard_set("pw", "pt", max(ntp, nt3))
                if npairs > 0:
                    P.guard_set("pstr", "pt", max(ntp, nt4))
                P.wait("vector", "pt", max(nt3, nt4))
                P.guard_wait("vector", "den")
                nv3 = P.op("vector", lambda e, hh=hh: e.tensor_reduce(
                    den[:, hh, :], psD1[:, :].rearrange("a (w q) -> a q w", q=64),
                    AXX, ALU.add), "pv")
                P.guard_set("psD1", "pv", nv3)
                if npairs > 0:
                    P.guard_wait("vector", "den2")
                    nv4 = P.op("vector", lambda e, npairs=npairs: e.tensor_reduce(
                        den2[:, :], psD2[:, :npairs * 64].rearrange("a (w q) -> a q w", q=64),
                        AXX, ALU.add), "pv")
                    P.guard_set("psD2", "pv", nv4)
                    nv4b = P.op("vector", lambda e, hh=hh: e.tensor_tensor(
                        den[:, hh, :], den[:, hh, :], den2[:, :], ALU.add), "pv")
                    P.guard_set("den2", "pv", nv4b)
            nlast = P.cnt["pt"]
            P.wait("vector", "pt", nlast)
            nv5 = P.op("vector", lambda e: e.reciprocal(denr[:, :], den[:, :, :].rearrange("a b c -> a (b c)")), "pv")
            P.wait("tensor", "pv", nv5)
            P.guard_wait("tensor", "psB")
            nt5 = P.op("tensor", lambda e: e.matmul(
                psB[:, 0:256], P.sb["ones1_f"][:, :], denr[:, :], start=True, stop=True), "pt")
            P.wait("vector", "pt", nt5)
            nv6 = P.op("vector", lambda e: e.tensor_copy(rb[:, :], psB[:, 0:256]), "pv")
            P.guard_set("psB", "pv", nv6)
            nv7 = P.op("vector", lambda e, g=g, j=j: e.tensor_tensor(
                ctx[:, 4 * g:4 * g + 4, j * 64:(j + 1) * 64],
                psC[:, 0:256].rearrange("p (h q) -> p h q", q=64),
                rb[:, :].rearrange("p (h q) -> p h q", q=64), ALU.mult), "pv")
            P.guard_set("psC", "pv", nv7)
            P.guard_set("kwin", "pt", P.cnt["pt"])
            P.guard_set("vwin", "pt", P.cnt["pt"])
        P.guard_set("q_sb", "pt", P.cnt["pt"])
        P.guard_set("kstr", "pt", P.cnt["pt"])
        P.guard_set("vstr", "pt", P.cnt["pt"])

    nctx = P.cnt["pv"]
    P.wait("tensor", "pv", nctx)
    for oc in range(NKC):
        par = oc % 2
        nw = phase_wstripe_T(P, wd, oc * 128, wf, wb, wt[:, par], idb_sb, pbT, f"s{par}")
        P.wait("tensor", f"s{par}v", nw)
        for tt in range(2):
            bank = psS if tt == 0 else psR
            bkey = "psS" if tt == 0 else "psR"
            P.guard_wait("tensor", bkey)
            nt = None
            for kc in range(NH):
                nt = P.op("tensor", lambda e, kc=kc, tt=tt, bank=bank, par=par: e.matmul(
                    bank[:, :], wt[:, par, kc, :], ctx[:, kc, tt * 512:(tt + 1) * 512],
                    start=(kc == 0), stop=(kc == NH - 1)), "pt")
            P.guard_set(f"s{par}_wtv", "pt", nt)
            P.guard_wait("sync", f"xsl{tt}")
            nxd = P.dma("sync", xsl[:, tt, :], x[:, oc, tt * 512:(tt + 1) * 512], f"xd{tt}")
            P.wait("vector", "pt", nt)
            P.wait("vector", f"xd{tt}", nxd)
            P.guard_wait("vector", f"xo{tt}")
            nv = P.op("vector", lambda e, oc=oc, tt=tt, bank=bank: e.scalar_tensor_tensor(
                xo[:, tt, :], bank[:, :], bd_sb[:, oc:oc + 1], xsl[:, tt, :],
                ALU.add, ALU.add), "pv")
            P.guard_set(bkey, "pv", nv)
            P.guard_set(f"xsl{tt}", "pv", nv)
            P.wait("sync", "pv", nv)
            nod = P.dma("sync", x1[:, oc, tt * 512:(tt + 1) * 512], xo[:, tt, :], "od")
            P.guard_set(f"xo{tt}", "od", nod)
    P.wait("sync", "od", P.cnt["od"])
    return P


def host_l3_inputs(c, inputs, x_fm, q_fm, K_full, V_full):
    mwin, mstr = build_l0_masks(c)
    sh = c * 64
    K_sh = np.zeros_like(K_full)
    K_sh[:, :, :SEQ - sh] = K_full[:, :, sh:]
    V_sh = np.zeros_like(V_full)
    V_sh[:, :SEQ - sh, :] = V_full[:, sh:, :]
    return {
        "x_fm": x_fm,
        "q_fm": q_fm,
        "K_full": K_full,
        "V_full": V_full,
        "K_sh": K_sh,
        "V_sh": V_sh,
        "w_dense": np.asarray(inputs["L0_attn_w_dense"], np.float32),
        "bdense": np.ascontiguousarray(
            np.asarray(inputs["L0_attn_b_dense"], np.float32).reshape(32, 128).T),
        "m_win": mwin.astype(BF),
        "m_str": mstr.astype(BF),
        "c_causal": np.concatenate([np.ones((64, 64), np.float32),
                                    causal_mask64()], axis=0).astype(BF),
        "c_ident_bf": np.eye(128, dtype=np.float32).astype(BF),
        **common_host(),
    }



def build_l4_ffn0():
    P = Prog()
    x = P.din("x_fm", [128, NKC, 1024], F32)
    WgT = P.din("WgT", [4096, I_FFN], BF16)
    WuT = P.din("WuT", [4096, I_FFN], BF16)
    WdT = P.din("WdT", [I_FFN, 4096], BF16)
    x2 = P.dout("x2_fm", [128, NKC, 1024], F32)
    h_dram = P.nc.dram_tensor("h_scr", [NIC, 128, 1024], BF16).ap()

    alloc_common(P)
    alloc_ln_scratch(P, 128)
    x_t = P.sbuf("x_t", [128, NKC, 128], F32)
    xn = P.sbuf("xn", [128, NKC, 1024], BF16)
    wgt = P.sbuf("wgt", [128, 2, NKC, 128], BF16)
    wut = P.sbuf("wut", [128, 2, NKC, 128], BF16)
    g_sb = P.sbuf("g_sb", [128, 512], BF16)
    u_sb = P.sbuf("u_sb", [128, 512], BF16)
    h_sb = P.sbuf("h_sb", [128, 2, 512], BF16)
    wdt = P.sbuf("wdt", [128, 2, 8, 512], BF16)
    h_ld = P.sbuf("h_ld", [128, 2, 8, 1024], BF16)
    xsl = P.sbuf("xsl", [128, 2, 512], F32)
    xo = P.sbuf("xo", [128, 2, 512], F32)

    pb = [P.psum(f"pb{i}", [128, 512], F32) for i in range(4)]
    pbG = P.psum("pbG", [128, 512], F32)
    pbU = P.psum("pbU", [128, 512], F32)
    pbD = P.psum("pbD", [128, 512], F32)
    pbB8 = P.psum("pbB8", [128, 512], F32)

    P.sem_new("c0", "xd", "wg0", "wg1", "wu0", "wu1", "pt", "pa", "pv",
              "hd", "ho0", "ho1", "wd0", "wd1", "hl0", "hl1", "xs0", "xs1", "od")
    n0 = load_common(P)
    for e in ("scalar", "vector", "tensor"):
        P.wait(e, "c0", n0)

    for st_ in range(8):
        P.guard_wait("sync", "x_t")
        nxd = P.dma("sync", x_t[:, :, :], x[:, :, st_ * 128:(st_ + 1) * 128], "xd")
        for e in ("scalar", "tensor", "vector"):
            P.wait(e, "xd", nxd)
        nv = phase_ln(P, x_t[:, :, :], 128, xn[:, :, st_ * 128:(st_ + 1) * 128],
                      f"ln{st_}", pb[0][0:1, :128], pb[1][0:1, :128],
                      pb[2][:, :128], pb[3][:, :128])
        P.guard_set("x_t", f"ln{st_}v", nv)
    P.wait("tensor", "ln7v", P.cnt["ln7v"])

    for ic in range(NIC):
        par = ic % 2
        P.guard_wait("sync", f"wgt{par}")
        nwg = P.dma("sync", wgt[:, par],
                    WgT[:, ic * 128:(ic + 1) * 128].rearrange("(a p) b -> p a b", p=128),
                    f"wg{par}")
        P.guard_wait("sync", f"wut{par}")
        nwu = P.dma("sync", wut[:, par],
                    WuT[:, ic * 128:(ic + 1) * 128].rearrange("(a p) b -> p a b", p=128),
                    f"wu{par}")
        P.wait("tensor", f"wg{par}", nwg)
        P.wait("tensor", f"wu{par}", nwu)
        for tt in range(2):
            P.guard_wait("tensor", "pbG")
            ntg = None
            for kc in range(NKC):
                ntg = P.op("tensor", lambda e, kc=kc, tt=tt, par=par: e.matmul(
                    pbG[:, :], wgt[:, par, kc, :], xn[:, kc, tt * 512:(tt + 1) * 512],
                    start=(kc == 0), stop=(kc == NKC - 1)), "pt")
            P.guard_wait("tensor", "pbU")
            ntu = None
            for kc in range(NKC):
                ntu = P.op("tensor", lambda e, kc=kc, tt=tt, par=par: e.matmul(
                    pbU[:, :], wut[:, par, kc, :], xn[:, kc, tt * 512:(tt + 1) * 512],
                    start=(kc == 0), stop=(kc == NKC - 1)), "pt")
            P.guard_set(f"wgt{par}", "pt", ntg)
            P.guard_set(f"wut{par}", "pt", ntu)
            P.wait("scalar", "pt", ntg)
            P.guard_wait("scalar", "g_sb")
            na = P.op("scalar", lambda e: e.activation(g_sb[:, :], pbG[:, :], AF.Silu), "pa")
            P.guard_set("pbG", "pa", na)
            P.wait("vector", "pt", ntu)
            P.guard_wait("vector", "u_sb")
            nv = P.op("vector", lambda e: e.tensor_copy(u_sb[:, :], pbU[:, :]), "pv")
            P.guard_set("pbU", "pv", nv)
            P.wait("vector", "pa", na)
            P.guard_wait("vector", f"h_sb{tt}")
            nv2 = P.op("vector", lambda e, tt=tt: e.tensor_tensor(
                h_sb[:, tt, :], g_sb[:, :], u_sb[:, :], ALU.mult), "pv")
            P.guard_set("g_sb", "pv", nv2)
            P.guard_set("u_sb", "pv", nv2)
            P.wait("sync", "pv", nv2)
            nho = P.dma("sync", h_dram[ic, :, tt * 512:(tt + 1) * 512], h_sb[:, tt, :],
                        f"ho{tt}")
            P.guard_set(f"h_sb{tt}", f"ho{tt}", nho)
    nh_all0, nh_all1 = P.cnt["ho0"], P.cnt["ho1"]

    sc_list = []
    icx = 0
    while icx < NIC:
        sc_list.append((icx, min(8, NIC - icx)))
        icx += min(8, NIC - icx)
    P.wait("sync", "ho0", nh_all0)
    P.wait("sync", "ho1", nh_all1)
    P.wait("tensor", "pa", P.cnt["pa"])
    P.wait("tensor", "pv", P.cnt["pv"])
    pbDD = [pb[0], pb[1], pb[2], pb[3], pbG, pbU, pbD, pbB8]
    for og in range(8):
        for bi in range(8):
            P.guard_wait("tensor", f"pbDD{bi}")
        for si, (ic0, nic8) in enumerate(sc_list):
            par = si % 2
            P.guard_wait("sync", f"h_ld{par}")
            nhl = P.dma("sync", h_ld[:, par, :nic8],
                        h_dram[ic0:ic0 + nic8].rearrange("a p b -> p a b"), f"hl{par}")
            P.guard_wait("sync", f"wdt{par}")
            nwd = P.dma("sync", wdt[:, par, :nic8],
                        WdT[ic0 * 128:(ic0 + nic8) * 128, og * 512:(og + 1) * 512].rearrange(
                            "(a p) b -> p a b", p=128), f"wd{par}")
            P.wait("tensor", f"wd{par}", nwd)
            P.wait("tensor", f"hl{par}", nhl)
            for oi in range(4):
                for i8 in range(nic8):
                    for tt in range(2):
                        first = (si == 0 and i8 == 0)
                        last = (si == len(sc_list) - 1 and i8 == nic8 - 1)
                        bi = oi * 2 + tt
                        nt = P.op("tensor", lambda e, i8=i8, tt=tt, par=par, oi=oi,
                                  first=first, last=last, bi=bi: e.matmul(
                            pbDD[bi][:, :], wdt[:, par, i8, oi * 128:(oi + 1) * 128],
                            h_ld[:, par, i8, tt * 512:(tt + 1) * 512],
                            start=first, stop=last, skip_group_check=True), "pt")
            P.guard_set(f"wdt{par}", "pt", P.cnt["pt"])
            P.guard_set(f"h_ld{par}", "pt", P.cnt["pt"])
        for oi in range(4):
            oc = og * 4 + oi
            for tt in range(2):
                bi = oi * 2 + tt
                P.guard_wait("sync", f"xsl{tt}")
                nxs = P.dma("sync", xsl[:, tt, :], x[:, oc, tt * 512:(tt + 1) * 512], f"xs{tt}")
                P.wait("vector", "pt", P.cnt["pt"])
                P.wait("vector", f"xs{tt}", nxs)
                P.guard_wait("vector", f"xo{tt}")
                nv = P.op("vector", lambda e, tt=tt, bi=bi: e.tensor_tensor(
                    xo[:, tt, :], pbDD[bi][:, :], xsl[:, tt, :], ALU.add), "pv")
                P.guard_set(f"pbDD{bi}", "pv", nv)
                P.guard_set(f"xsl{tt}", "pv", nv)
                P.wait("sync", "pv", nv)
                nod = P.dma("sync", x2[:, oc, tt * 512:(tt + 1) * 512], xo[:, tt, :], "od")
                P.guard_set(f"xo{tt}", "od", nod)
    P.wait("sync", "od", P.cnt["od"])
    return P


def host_l4_inputs(c, x1_fm, WgT, WuT, WdT):
    return {"x_fm": x1_fm, "WgT": WgT, "WuT": WuT, "WdT": WdT, **common_host()}



def build_l5_qkv1():
    P = Prog()
    x = P.din("x_fm", [128, NKC, 1024], F32)
    wq = P.din("wq", [4096, 4096], F32)
    wk = P.din("wk", [4096, 128], F32)
    wv = P.din("wv", [4096, 128], F32)
    identb = P.din("c_ident_bf", [128, 128], BF16)
    q_out = P.dout("q1_fm", [32, 128, 1024], BF16)
    k_out = P.dout("k1_fm", [128, 1024], BF16)
    v_out = P.dout("v1_tm", [1024, 128], BF16)

    alloc_common(P)
    alloc_ln_scratch(P, 256)
    x_t = P.sbuf("x_t", [128, NKC, 256], F32)
    xn = P.sbuf("xn", [128, NKC, 1024], BF16)
    wt = P.sbuf("wt", [128, 2, NKC, 128], BF16)
    wkv = P.sbuf("wkv", [128, 2, NKC, 128], BF16)
    obuf = P.sbuf("obuf", [128, 2, 512], BF16)
    vtm = P.sbuf("vtm", [128, 2, 4, 128], BF16)
    idb_sb = P.sbuf("idb_sb", [128, 128], BF16)

    pb = [P.psum(f"pb{i}", [128, 512], F32) for i in range(4)]
    pbV = P.psum("pbV", [128, 512], BF16)

    P.sem_new("c0", "ld", "xd", "wq0", "wq1", "wkv0", "wkv1", "pt", "pv", "od")
    n0 = load_common(P)
    nld = P.dma("sync", idb_sb[:, :], identb, "ld")
    for e in ("scalar", "vector", "tensor"):
        P.wait(e, "c0", n0)
        P.wait(e, "ld", nld)

    for st_ in range(4):
        P.guard_wait("sync", "x_t")
        nxd = P.dma("sync", x_t[:, :, :], x[:, :, st_ * 256:(st_ + 1) * 256], "xd")
        for e in ("scalar", "tensor", "vector"):
            P.wait(e, "xd", nxd)
        nv = phase_ln(P, x_t[:, :, :], 256, xn[:, :, st_ * 256:(st_ + 1) * 256],
                      f"ln{st_}", pb[0][0:1, :256], pb[1][0:1, :256],
                      pb[2][:, :256], pb[3][:, :256])
        P.guard_set("x_t", f"ln{st_}v", nv)
    P.wait("tensor", "ln3v", P.cnt["ln3v"])

    for oc in range(NKC):
        par = oc % 2
        P.guard_wait("gpsimd", f"wt{par}")
        nw = P.dma("gpsimd", wt[:, par],
                   wq[:, oc * 128:(oc + 1) * 128].rearrange("(a p) b -> p a b", p=128),
                   f"wq{par}")
        P.wait("tensor", f"wq{par}", nw)
        for tt in range(2):
            bank = pb[tt]
            P.guard_wait("tensor", f"pb{tt}")
            nt = None
            for kc in range(NKC):
                nt = P.op("tensor", lambda e, kc=kc, tt=tt, par=par, bank=bank: e.matmul(
                    bank[:, :], wt[:, par, kc, :], xn[:, kc, tt * 512:(tt + 1) * 512],
                    start=(kc == 0), stop=(kc == NKC - 1)), "pt")
            P.guard_set(f"wt{par}", "pt", nt)
            P.wait("vector", "pt", nt)
            P.guard_wait("vector", f"obuf{tt}")
            nv = P.op("vector", lambda e, tt=tt, bank=bank: e.tensor_copy(
                obuf[:, tt, :], bank[:, :]), "pv")
            P.guard_set(f"pb{tt}", "pv", nv)
            P.wait("sync", "pv", nv)
            nod = P.dma("sync", q_out[oc, :, tt * 512:(tt + 1) * 512], obuf[:, tt, :], "od")
            P.guard_set(f"obuf{tt}", "od", nod)

    for which, wsrc, who in (("k", wk, 0), ("v", wv, 1)):
        par = who
        P.guard_wait("gpsimd", f"wkv{par}")
        nw = P.dma("gpsimd", wkv[:, par],
                   wsrc.rearrange("(a p) b -> p a b", p=128), f"wkv{par}")
        P.wait("tensor", f"wkv{par}", nw)
        for tt in range(2):
            bank = pb[2 + tt]
            P.guard_wait("tensor", f"pb{2 + tt}")
            nt = None
            for kc in range(NKC):
                nt = P.op("tensor", lambda e, kc=kc, tt=tt, par=par, bank=bank: e.matmul(
                    bank[:, :], wkv[:, par, kc, :], xn[:, kc, tt * 512:(tt + 1) * 512],
                    start=(kc == 0), stop=(kc == NKC - 1)), "pt")
            P.guard_set(f"wkv{par}", "pt", nt)
            P.wait("vector", "pt", nt)
            P.guard_wait("vector", f"obuf{tt}")
            nv = P.op("vector", lambda e, tt=tt, bank=bank: e.tensor_copy(
                obuf[:, tt, :], bank[:, :]), "pv")
            P.guard_set(f"pb{2 + tt}", "pv", nv)
            if which == "k":
                P.wait("sync", "pv", nv)
                nod = P.dma("sync", k_out[:, tt * 512:(tt + 1) * 512], obuf[:, tt, :], "od")
                P.guard_set(f"obuf{tt}", "od", nod)
            else:
                P.wait("tensor", "pv", nv)
                P.guard_wait("tensor", "pbV")
                ntv = None
                for i in range(4):
                    ntv = P.op("tensor", lambda e, tt=tt, i=i: e.transpose(
                        pbV[:, i * 128:(i + 1) * 128],
                        obuf[:, tt, i * 128:(i + 1) * 128], idb_sb[:, :]), "pt")
                P.guard_set(f"obuf{tt}", "pt", ntv)
                P.wait("vector", "pt", ntv)
                P.guard_wait("vector", f"vtm{tt}")
                nv2 = P.op("vector", lambda e, tt=tt: e.tensor_copy(
                    vtm[:, tt, :, :].rearrange("p a b -> p (a b)"), pbV[:, :]), "pv")
                P.guard_set("pbV", "pv", nv2)
                P.wait("sync", "pv", nv2)
                nod = P.dma("sync",
                            v_out[tt * 512:(tt + 1) * 512, :].rearrange("(a p) b -> p a b", p=128),
                            vtm[:, tt, :, :], "od")
                P.guard_set(f"vtm{tt}", "od", nod)
    P.wait("sync", "od", P.cnt["od"])
    return P


def host_l5_inputs(c, inputs, x2_fm):
    return {
        "x_fm": x2_fm,
        "wq": np.asarray(inputs["L1_attn_wq"], np.float32),
        "wk": np.asarray(inputs["L1_attn_wk"], np.float32),
        "wv": np.asarray(inputs["L1_attn_wv"], np.float32),
        "c_ident_bf": np.eye(128, dtype=np.float32).astype(BF),
        **common_host(),
    }



def build_l6_attn1():
    P = Prog()
    x = P.din("x_fm", [128, NKC, 1024], F32)
    q_in = P.din("q1_fm", [32, 128, 1024], BF16)
    K1 = P.din("K1", [128, 8192], BF16)
    V1 = P.din("V1", [8192, 128], BF16)
    wo = P.din("wo", [4096, 4096], F32)
    m1 = P.din("m_l1", [64, 128, 128], BF16)
    x3 = P.dout("x3_fm", [128, NKC, 1024], F32)

    alloc_common(P)
    ctx = P.sbuf("ctx", [128, NH, 1024], BF16)
    k_sb = P.sbuf("k_sb", [128, 64, 128], BF16)
    v_sb = P.sbuf("v_sb", [128, 64, 128], BF16)
    q_sb = P.sbuf("q_sb", [128, 2, 1024], BF16)
    m_sb = P.sbuf("m_sb", [128, 64, 128], BF16)
    pw = P.sbuf("pw", [128, 2, 2, 512], BF16)
    den = P.sbuf("den", [1, 1024], F32)
    denr = P.sbuf("denr", [1, 1024], F32)
    rb = P.sbuf("rb", [128, 2, 512], F32)
    wt = P.sbuf("wt", [128, 2, NKC, 128], BF16)
    xsl = P.sbuf("xsl", [128, 2, 512], F32)
    xo = P.sbuf("xo", [128, 2, 512], F32)

    psS = [P.psum(f"psS{i}", [128, 512], F32) for i in range(2)]
    psC = [P.psum(f"psC{i}", [128, 512], F32) for i in range(2)]
    psD = [P.psum(f"psD{i}", [1, 512], F32) for i in range(2)]
    psB = P.psum("psB", [128, 512], F32)
    SM = float(1.0 / np.sqrt(128.0))

    P.sem_new("c0", "ld", "qd0", "qd1", "pt", "pa", "pv", "od0", "od1", "xs0", "xs1",
              "wo0", "wo1")
    n0 = load_common(P)
    P.dma("sync", k_sb[:, :, :], K1.rearrange("p (a b) -> p a b", b=128), "ld")
    P.dma("sync", v_sb[:, :, :], V1.rearrange("(a p) b -> p a b", p=128), "ld")
    nld = P.dma("sync", m_sb[:, :, :], m1.rearrange("a p b -> p a b"), "ld")
    for e in ("scalar", "vector", "tensor"):
        P.wait(e, "c0", n0)
        P.wait(e, "ld", nld)

    ops_struct = []
    for t in range(64):
        j0 = l1_j0u(t)
        col0 = j0 * 64
        cc = col0
        ci = 0
        while cc < 1024:
            w = min(512, 1024 - cc)
            for half in range(2):
                h0, h1 = half * 512, (half + 1) * 512
                lo, hi = max(cc, h0), min(cc + w, h1)
                if lo < hi:
                    ops_struct.append((t, ci, cc, w, half, lo, hi))
            cc += w
            ci += 1
    npv = [sum(1 for o in ops_struct if o[4] == h) for h in range(2)]

    for qh in range(NH):
        par = qh % 2
        P.guard_wait("sync", f"q_sb{par}")
        nqd = P.dma("sync", q_sb[:, par, :], q_in[qh], f"qd{par}")
        P.wait("tensor", f"qd{par}", nqd)
        pvk = [0, 0]

        def chunks_of(t):
            j0 = l1_j0u(t)
            col0 = j0 * 64
            ch = []
            cc = col0
            while cc < 1024:
                w = min(512, 1024 - cc)
                ch.append((cc, w))
                cc += w
            return ch

        def emit_scores(t):
            tp = t % 2
            chunks = chunks_of(t)
            ncols = 1024 - chunks[0][0]
            nts, nas = [], []
            for ci, (cc, w) in enumerate(chunks):
                bank = psS[ci % 2]
                P.guard_wait("tensor", f"psS{ci % 2}")
                nt = P.op("tensor", lambda e, t=t, cc=cc, w=w, par=par, bank=bank: e.matmul(
                    bank[:, :w], k_sb[:, t, :], q_sb[:, par, cc:cc + w],
                    start=True, stop=True, skip_group_check=True), "pt")
                nts.append(nt)
                P.wait("scalar", "pt", nt)
                P.guard_wait("scalar", f"pw{tp}{ci % 2}")
                na = P.op("scalar", lambda e, ci=ci, w=w, tp=tp, bank=bank: e.activation(
                    pw[:, tp, ci % 2, :w], bank[:, :w], AF.Exp, scale=SM), "pa")
                P.guard_set(f"psS{ci % 2}", "pa", na)
                nas.append(na)
            P.wait("vector", "pa", nas[0])
            mw = min(128, ncols)
            nv = P.op("vector", lambda e, t=t, mw=mw, tp=tp: e.tensor_tensor(
                pw[:, tp, 0, :mw], pw[:, tp, 0, :mw], m_sb[:, t, :mw], ALU.mult), "pv")
            return (t, chunks, nas, nv)

        def emit_pv(info):
            t, chunks, nas, nv = info
            tp = t % 2
            for ci, (cc, w) in enumerate(chunks):
                if ci == 0:
                    P.wait("tensor", "pv", nv)
                else:
                    P.wait("tensor", "pa", nas[ci])
                for half in range(2):
                    h0, h1 = half * 512, (half + 1) * 512
                    lo, hi = max(cc, h0), min(cc + w, h1)
                    if lo >= hi:
                        continue
                    fc = (pvk[half] == 0)
                    lastf = (pvk[half] == npv[half] - 1)
                    pvk[half] += 1
                    if fc:
                        P.guard_wait("tensor", f"psC{half}")
                        P.guard_wait("tensor", f"psD{half}")
                    ntp = P.op("tensor", lambda e, t=t, lo=lo, hi=hi, half=half, ci=ci,
                               fc=fc, lastf=lastf, cc=cc, tp=tp: e.matmul(
                        psC[half][:, lo - half * 512:hi - half * 512], v_sb[:, t, :],
                        pw[:, tp, ci % 2, lo - cc:hi - cc],
                        start=fc, stop=lastf, skip_group_check=True), "pt")
                    ntd = P.op("tensor", lambda e, lo=lo, hi=hi, ci=ci, cc=cc, half=half,
                               fc=fc, lastf=lastf, tp=tp: e.matmul(
                        psD[half][:, lo - half * 512:hi - half * 512],
                        P.sb["ones_bf"][:, :], pw[:, tp, ci % 2, lo - cc:hi - cc],
                        start=fc, stop=lastf, skip_group_check=True), "pt")
                    P.guard_set(f"pw{tp}{ci % 2}", "pt", max(ntp, ntd))

        prev = None
        for t in range(64):
            info = emit_scores(t)
            if prev is not None:
                emit_pv(prev)
            prev = info
        emit_pv(prev)
        nlast = P.cnt["pt"]
        P.wait("vector", "pt", nlast)
        P.guard_wait("vector", "den")
        P.op("vector", lambda e: e.tensor_copy(den[:, 0:512], psD[0][:, :]), "pv")
        nv5 = P.op("vector", lambda e: e.tensor_copy(den[:, 512:1024], psD[1][:, :]), "pv")
        P.guard_set("psD0", "pv", nv5)
        P.guard_set("psD1", "pv", nv5)
        P.wait("vector", "pv", nv5)
        nv6 = P.op("vector", lambda e: e.reciprocal(denr[:, :], den[:, :]), "pv")
        P.wait("tensor", "pv", nv6)
        for half in range(2):
            P.guard_wait("tensor", "psB")
            nt5 = P.op("tensor", lambda e, half=half: e.matmul(
                psB[:, :], P.sb["ones1_f"][:, :], denr[:, half * 512:(half + 1) * 512],
                start=True, stop=True, skip_group_check=True), "pt")
            P.wait("vector", "pt", nt5)
            P.guard_wait("vector", f"rb{half}")
            nv7 = P.op("vector", lambda e, half=half: e.tensor_copy(
                rb[:, half, :], psB[:, :]), "pv")
            P.guard_set("psB", "pv", nv7)
            P.wait("vector", "pv", nv7)
            nv8 = P.op("vector", lambda e, qh=qh, half=half: e.tensor_tensor(
                ctx[:, qh, half * 512:(half + 1) * 512], psC[half][:, :],
                rb[:, half, :], ALU.mult), "pv")
            P.guard_set(f"psC{half}", "pv", nv8)
            P.guard_set(f"rb{half}", "pv", nv8)
        P.guard_set(f"q_sb{par}", "pt", P.cnt["pt"])

    nctx = P.cnt["pv"]
    P.wait("tensor", "pv", nctx)
    for oc in range(NKC):
        par = oc % 2
        P.guard_wait("gpsimd", f"wt{par}")
        nw = P.dma("gpsimd", wt[:, par],
                   wo[:, oc * 128:(oc + 1) * 128].rearrange("(a p) b -> p a b", p=128),
                   f"wo{par}")
        P.wait("tensor", f"wo{par}", nw)
        for tt in range(2):
            bank = psS[tt]
            P.guard_wait("tensor", f"psS{tt}")
            nt = None
            for kc in range(NKC):
                nt = P.op("tensor", lambda e, kc=kc, tt=tt, par=par, bank=bank: e.matmul(
                    bank[:, :], wt[:, par, kc, :], ctx[:, kc, tt * 512:(tt + 1) * 512],
                    start=(kc == 0), stop=(kc == NKC - 1)), "pt")
            P.guard_set(f"wt{par}", "pt", nt)
            P.guard_wait("sync", f"xsl{tt}")
            nxs = P.dma("sync", xsl[:, tt, :], x[:, oc, tt * 512:(tt + 1) * 512], f"xs{tt}")
            P.wait("vector", "pt", nt)
            P.wait("vector", f"xs{tt}", nxs)
            P.guard_wait("vector", f"xo{tt}")
            nv = P.op("vector", lambda e, tt=tt, bank=bank: e.tensor_tensor(
                xo[:, tt, :], bank[:, :], xsl[:, tt, :], ALU.add), "pv")
            P.guard_set(f"psS{tt}", "pv", nv)
            P.guard_set(f"xsl{tt}", "pv", nv)
            P.wait("sync", "pv", nv)
            nod = P.dma("sync", x3[:, oc, tt * 512:(tt + 1) * 512], xo[:, tt, :], f"od{tt}")
            P.guard_set(f"xo{tt}", f"od{tt}", nod)
    P.wait("sync", "od0", P.cnt["od0"])
    P.wait("sync", "od1", P.cnt["od1"])
    return P


def host_l6_inputs(c, inputs, x2_fm, q1_fm, K1, V1):
    return {
        "x_fm": x2_fm,
        "q1_fm": q1_fm,
        "K1": K1,
        "V1": V1,
        "wo": np.asarray(inputs["L1_attn_wo"], np.float32),
        "m_l1": build_l1_masks(c).astype(BF),
        **common_host(),
    }



def build_l7_ffn1():
    P = Prog()
    x = P.din("x_fm", [128, NKC, 1024], F32)
    bw = P.din("base_w", [4096, 4096], F32)
    bb = P.din("base_b", [128, 32], F32)
    A = P.din("A_w", [64, 4096], F32)
    Bm = P.din("B_w", [16, 4096, 64], F32)
    rw = P.din("router_w", [16, 4096], F32)
    identb = P.din("c_ident_bf", [128, 128], BF16)
    x4 = P.dout("x4_fm", [128, NKC, 1024], F32)

    alloc_common(P)
    alloc_ln_scratch(P, 256)
    x_t = P.sbuf("x_t", [128, NKC, 256], F32)
    xn = P.sbuf("xn", [128, NKC, 1024], BF16)
    at_sb = P.sbuf("at_sb", [128, NKC, 64], BF16)
    rt_sb = P.sbuf("rt_sb", [128, NKC, 16], BF16)
    bf_ = P.sbuf("bf_", [128, 16, 64], F32)
    bb16 = P.sbuf("bb16", [128, 16, 64], BF16)
    br_sb = P.sbuf("br_sb", [128, 2, 8, 128], BF16)
    brw = P.sbuf("brw", [128, 128], BF16)
    logit = P.sbuf("logit", [128, 8, 16], F32)
    rh_sb = P.sbuf("rh_sb", [128, 8, 64], F32)
    tk = P.sbuf("tk", [128, 10, 16], F32)
    tks = P.sbuf("tks", [128, 8, 1], F32)
    comb = P.sbuf("comb", [128, 16], F32)
    C_sb = P.sbuf("C_sb", [128, 1024], BF16)
    ct_sb = P.sbuf("ct_sb", [128, 8, 1024], BF16)
    wf = P.sbuf("wf", [128, 4096], F32)
    wb = P.sbuf("wb", [128, 4096], BF16)
    af = wf[0:64, :]
    ab = wb[0:64, :]
    rf = wf[0:16, :]
    rb_ = wb[0:16, :]
    wt = P.sbuf("wt", [128, 2, NKC, 128], BF16)
    bb_sb = P.sbuf("bb_sb", [128, 32], F32)
    idb_sb = P.sbuf("idb_sb", [128, 128], BF16)
    xsl = P.sbuf("xsl", [128, 2, 512], F32)
    xo = P.sbuf("xo", [128, 2, 512], F32)

    pb = [P.psum(f"pb{i}", [128, 512], F32) for i in range(4)]
    pbT = [P.psum(f"pbT{i}", [128, 1024], BF16) for i in range(2)]
    pbL = P.psum("pbL", [128, 512], F32)
    br_dram = P.nc.dram_tensor("br_scr", [1024, 4096], BF16).ap()

    P.sem_new("c0", "ld", "xd", "pp", "pa", "pv", "pt", "od", "xs0", "xs1", "brd0", "brd1")
    n0 = load_common(P)
    P.dma("sync", af, A, "ld")
    P.dma("sync", bb_sb[:, :], bb, "ld")
    nld = P.dma("sync", idb_sb[:, :], identb, "ld")
    for e in ("scalar", "vector", "tensor"):
        P.wait(e, "c0", n0)
        P.wait(e, "ld", nld)

    na = P.op("scalar", lambda e: e.activation(ab, af, AF.Copy, bias=0.0), "pa")
    P.wait("tensor", "pa", na)
    for kc in range(NKC):
        bank = pbT[kc % 2]
        P.guard_wait("tensor", f"pbT{kc % 2}")
        nt = P.op("tensor", lambda e, kc=kc, bank=bank: e.transpose(
            bank[:, 0:64], ab[:, kc * 128:(kc + 1) * 128], idb_sb[0:64, 0:64]), "pt")
        P.wait("vector", "pt", nt)
        nv = P.op("vector", lambda e, kc=kc, bank=bank: e.tensor_copy(at_sb[:, kc, :], bank[:, 0:64]), "pv")
        P.guard_set(f"pbT{kc % 2}", "pv", nv)
    nlastA = P.cnt["pt"]
    P.wait("sync", "pt", nlastA)
    nrd = P.dma("sync", rf, rw, "ld")
    P.wait("scalar", "ld", nrd)
    P.wait("scalar", "pt", nlastA)
    na = P.op("scalar", lambda e: e.activation(rb_, rf, AF.Copy, bias=0.0), "pa")
    P.wait("tensor", "pa", na)
    for kc in range(NKC):
        bank = pbT[kc % 2]
        P.guard_wait("tensor", f"pbT{kc % 2}")
        nt = P.op("tensor", lambda e, kc=kc, bank=bank: e.transpose(
            bank[:, 0:16], rb_[:, kc * 128:(kc + 1) * 128], idb_sb[0:16, 0:16]), "pt")
        P.wait("vector", "pt", nt)
        nv = P.op("vector", lambda e, kc=kc, bank=bank: e.tensor_copy(rt_sb[:, kc, :], bank[:, 0:16]), "pv")
        P.guard_set(f"pbT{kc % 2}", "pv", nv)
    P.guard_set("s0_wf0", "pt", P.cnt["pt"])
    P.guard_set("s0_wf1", "pt", P.cnt["pt"])
    P.guard_set("s1_wf0", "pt", P.cnt["pt"])
    P.guard_set("s1_wf1", "pt", P.cnt["pt"])
    P.guard_set("s0_wb", "pt", P.cnt["pt"])
    P.guard_set("s1_wb", "pt", P.cnt["pt"])
    P.wait("tensor", "pv", P.cnt["pv"])
    for ee in range(16):
        for bh in range(2):
            P.guard_wait("sync", "bf_")
            nbd = P.dma("sync", bf_[:, :, :],
                        Bm[ee, bh * 2048:(bh + 1) * 2048, :].rearrange("(a p) b -> p a b", p=128),
                        "xd")
            P.wait("scalar", "xd", nbd)
            P.guard_wait("scalar", "bb16")
            na = P.op("scalar", lambda e: e.activation(bb16[:, :, :], bf_[:, :, :], AF.Copy, bias=0.0), "pa")
            P.guard_set("bf_", "pa", na)
            P.wait("tensor", "pa", na)
            for dl in range(16):
                dt = bh * 16 + dl
                bank = pbT[dt % 2]
                P.guard_wait("tensor", f"pbT{dt % 2}x")
                nt = P.op("tensor", lambda e, dl=dl, bank=bank: e.transpose(
                    bank[0:64, 0:128], bb16[:, dl, :].rearrange("p (a b) -> p a b", a=1)[:, 0, :],
                    idb_sb[:, :]), "pt")
                P.wait("vector", "pt", nt)
                P.guard_wait("vector", "brw")
                nv = P.op("vector", lambda e, bank=bank: e.tensor_copy(
                    brw[0:64, :], bank[0:64, 0:128]), "pv")
                P.guard_set(f"pbT{dt % 2}x", "pv", nv)
                P.wait("sync", "pv", nv)
                nbo = P.dma("sync", br_dram[ee * 64:(ee + 1) * 64, dt * 128:(dt + 1) * 128],
                            brw[0:64, :], "od")
                P.guard_set("brw", "od", nbo)
            P.guard_set("bb16", "pt", P.cnt["pt"])
    npr = P.cnt["pv"]

    for st_ in range(4):
        P.guard_wait("sync", "x_t")
        nxd = P.dma("sync", x_t[:, :, :], x[:, :, st_ * 256:(st_ + 1) * 256], "xd")
        for e in ("scalar", "tensor", "vector"):
            P.wait(e, "xd", nxd)
        nv = phase_ln(P, x_t[:, :, :], 256, xn[:, :, st_ * 256:(st_ + 1) * 256],
                      f"ln{st_}", pb[0][0:1, :256], pb[1][0:1, :256],
                      pb[2][:, :256], pb[3][:, :256])
        P.guard_set("x_t", f"ln{st_}v", nv)
    P.wait("tensor", "ln3v", P.cnt["ln3v"])

    for ti in range(8):
        P.guard_wait("tensor", "pbL")
        for kc in range(NKC):
            P.op("tensor", lambda e, kc=kc, ti=ti: e.matmul(
                pbL[:, 0:16], xn[:, kc, ti * 128:(ti + 1) * 128], rt_sb[:, kc, :],
                start=(kc == 0), stop=(kc == NKC - 1), skip_group_check=True), "pt")
            nt = P.op("tensor", lambda e, kc=kc, ti=ti: e.matmul(
                pbL[:, 16:80], xn[:, kc, ti * 128:(ti + 1) * 128], at_sb[:, kc, :],
                start=(kc == 0), stop=(kc == NKC - 1), skip_group_check=True), "pt")
        P.wait("vector", "pt", nt)
        P.op("vector", lambda e, ti=ti: e.tensor_copy(logit[:, ti, :], pbL[:, 0:16]))
        nv = P.op("vector", lambda e, ti=ti: e.tensor_copy(rh_sb[:, ti, :], pbL[:, 16:80]), "pv")
        P.guard_set("pbL", "pv", nv)
        nv = P.op("vector", lambda e, ti=ti: e.reduce_max(tks[:, 0, :], logit[:, ti, :], AXX, negate=True), "pv")
        P.wait("scalar", "pv", nv)
        na = P.op("scalar", lambda e, ti=ti: e.activation(
            tk[:, 0, :], logit[:, ti, :], AF.Exp, bias=tks[:, 0, :], scale=1.0), "pa")
        P.wait("vector", "pa", na)
        P.op("vector", lambda e: e.reduce_sum(tks[:, 1, :], tk[:, 0, :], AXX))
        P.op("vector", lambda e: e.reciprocal(tks[:, 2, :], tks[:, 1, :]))
        P.op("vector", lambda e: e.tensor_scalar(tk[:, 1, :], tk[:, 0, :], tks[:, 2, :], None, ALU.mult))
        P.op("vector", lambda e: e.reduce_max(tks[:, 3, :], tk[:, 1, :], AXX))
        P.op("vector", lambda e: e.tensor_scalar(tk[:, 2, :], tk[:, 1, :], tks[:, 3, :], None, ALU.is_equal))
        P.op("vector", lambda e: e.tensor_tensor(tk[:, 3, :], tk[:, 2, :], tk[:, 1, :], ALU.mult))
        P.op("vector", lambda e: e.tensor_tensor(tk[:, 4, :], tk[:, 1, :], tk[:, 3, :], ALU.subtract))
        P.op("vector", lambda e: e.reduce_max(tks[:, 4, :], tk[:, 4, :], AXX))
        P.op("vector", lambda e: e.tensor_scalar(tk[:, 5, :], tk[:, 1, :], tks[:, 4, :], None, ALU.is_ge))
        P.op("vector", lambda e: e.tensor_tensor(tks[:, 5, :], tks[:, 3, :], tks[:, 4, :], ALU.add))
        P.op("vector", lambda e: e.tensor_scalar(tks[:, 5, :], tks[:, 5, :], 1e-6, None, ALU.add))
        P.op("vector", lambda e: e.reciprocal(tks[:, 6, :], tks[:, 5, :]))
        P.op("vector", lambda e: e.tensor_scalar(tks[:, 6, :], tks[:, 6, :], 0.5, None, ALU.mult))
        P.op("vector", lambda e: e.tensor_tensor(tk[:, 6, :], tk[:, 1, :], tk[:, 5, :], ALU.mult))
        nv = P.op("vector", lambda e: e.tensor_scalar(comb[:, :], tk[:, 6, :], tks[:, 6, :], None, ALU.mult), "pv")
        P.guard_wait("vector", "C_sb")
        for ee in range(16):
            nv = P.op("vector", lambda e, ee=ee, ti=ti: e.tensor_scalar(
                C_sb[:, ee * 64:(ee + 1) * 64], rh_sb[:, ti, :], comb[:, ee:ee + 1],
                None, ALU.mult), "pv")
        P.wait("tensor", "pv", nv)
        for j in range(8):
            bank = pbT[j % 2]
            P.guard_wait("tensor", f"pbT{j % 2}y")
            nt = P.op("tensor", lambda e, ti=ti, j=j, bank=bank: e.transpose(
                bank[:, 0:128], C_sb[:, j * 128:(j + 1) * 128], idb_sb[:, :]), "pt")
            P.wait("vector", "pt", nt)
            nv = P.op("vector", lambda e, ti=ti, j=j, bank=bank: e.tensor_copy(
                ct_sb[:, j, ti * 128:(ti + 1) * 128], bank[:, 0:128]), "pv")
            P.guard_set(f"pbT{j % 2}y", "pv", nv)
        P.guard_set("C_sb", "pt", P.cnt["pt"])
    nct = P.cnt["pv"]
    P.wait("tensor", "pv", nct)

    for oc in range(NKC):
        par = oc % 2
        nw = phase_wstripe_T(P, bw, oc * 128, wf, wb, wt[:, par], idb_sb, pbT, f"s{par}")
        P.wait("tensor", f"s{par}v", nw)
        for tt in range(2):
            bank = pb[tt]
            P.guard_wait("tensor", f"pb{tt}")
            for kc in range(NKC):
                P.op("tensor", lambda e, kc=kc, tt=tt, par=par, bank=bank: e.matmul(
                    bank[:, :], wt[:, par, kc, :], xn[:, kc, tt * 512:(tt + 1) * 512],
                    start=(kc == 0), stop=False, skip_group_check=True), "pt")
            bpar = (oc * 2 + tt) % 2
            P.guard_wait("sync", f"br_sb{bpar}")
            nbr = P.dma("sync", br_sb[:, bpar],
                        br_dram[:, oc * 128:(oc + 1) * 128].rearrange("(a p) b -> p a b", p=128),
                        f"brd{bpar}")
            P.wait("tensor", f"brd{bpar}", nbr)
            nt = None
            for j in range(8):
                nt = P.op("tensor", lambda e, j=j, tt=tt, bpar=bpar, bank=bank: e.matmul(
                    bank[:, :], br_sb[:, bpar, j, :], ct_sb[:, j, tt * 512:(tt + 1) * 512],
                    start=False, stop=(j == 7), skip_group_check=True), "pt")
            P.guard_set(f"br_sb{bpar}", "pt", nt)
            P.guard_set(f"s{par}_wtv", "pt", nt)
            P.guard_wait("sync", f"xsl{tt}")
            nxs = P.dma("sync", xsl[:, tt, :], x[:, oc, tt * 512:(tt + 1) * 512], f"xs{tt}")
            P.wait("vector", "pt", nt)
            P.wait("vector", f"xs{tt}", nxs)
            P.guard_wait("vector", f"xo{tt}")
            nv = P.op("vector", lambda e, oc=oc, tt=tt, bank=bank: e.scalar_tensor_tensor(
                xo[:, tt, :], bank[:, :], bb_sb[:, oc:oc + 1], xsl[:, tt, :],
                ALU.add, ALU.add), "pv")
            P.guard_set(f"pb{tt}", "pv", nv)
            P.guard_set(f"xsl{tt}", "pv", nv)
            P.wait("sync", "pv", nv)
            nod = P.dma("sync", x4[:, oc, tt * 512:(tt + 1) * 512], xo[:, tt, :], "od")
            P.guard_set(f"xo{tt}", "od", nod)
    P.wait("sync", "od", P.cnt["od"])
    return P


def host_l7_inputs(c, inputs, x3_fm):
    return {
        "x_fm": x3_fm,
        "base_w": np.asarray(inputs["L1_ffn_base_w"], np.float32),
        "base_b": np.ascontiguousarray(
            np.asarray(inputs["L1_ffn_base_b"], np.float32).reshape(32, 128).T),
        "A_w": np.asarray(inputs["L1_ffn_A"], np.float32),
        "B_w": np.asarray(inputs["L1_ffn_B"], np.float32),
        "router_w": np.asarray(inputs["L1_ffn_router_w"], np.float32),
        "c_ident_bf": np.eye(128, dtype=np.float32).astype(BF),
        **common_host(),
    }



def build_l1_gather():
    P = Prog()
    cb = P.din("cb", [CB, 8], F32)
    idxg = P.din("idxg", [128, 11, 512], U32)
    idxu = P.din("idxu", [128, 11, 512], U32)
    idxd = P.din("idxd", [128, 11, 32, 16], U32)
    identb = P.din("c_ident_bf", [128, 128], BF16)
    wgT_o = P.dout("wgT_o", [4096, 1408], BF16)
    wuT_o = P.dout("wuT_o", [4096, 1408], BF16)
    wdT_o = P.dout("wdT_o", [1408, 4096], BF16)

    ig = P.sbuf("ig", [128, 11, 512], U32)
    iu = P.sbuf("iu", [128, 11, 512], U32)
    idn = P.sbuf("idn", [128, 11, 32, 16], U32)
    gout = P.sbuf("gout", [128, 2, 512, 8], BF16)
    wtb = P.sbuf("wtb", [128, 2, 32, 128], BF16)
    idb_sb = P.sbuf("idb_sb", [128, 128], BF16)

    pbT = [P.psum(f"pbT{i}", [128, 1024], BF16) for i in range(2)]

    P.sem_new("ld", "gt0", "gt1", "pt", "pv", "od0", "od1")
    P.dma("sync", ig[:, :, :], idxg, "ld")
    P.dma("sync", iu[:, :, :], idxu, "ld")
    P.dma("sync", idn[:, :, :, :], idxd, "ld")
    nld = P.dma("sync", idb_sb[:, :], identb, "ld")
    P.wait("gpsimd", "ld", nld)
    P.wait("tensor", "ld", nld)

    def emit_tile(rt_par, idx_ap_fn, ncalls, gsem):
        n = None
        for ccall in range(ncalls):
            n = P.inc(gsem, 16)
            s = P.sem[gsem]

            def call(e, ccall=ccall, rt_par=rt_par, idx_ap_fn=idx_ap_fn, s=s):
                return e.indirect_dma_start(
                    out=gout[:, rt_par, ccall, :], out_offset=None,
                    in_=cb[:, :],
                    in_offset=IOA(ap=idx_ap_fn(ccall), axis=0)).then_inc(s, 16)
            P.emit("gpsimd", call)
        return n

    def transpose_and_store(rt, par, out_dram, mode):
        nv = None
        for grp in range(4):
            bank = pbT[grp % 2]
            P.guard_wait("tensor", f"gpbT{grp % 2}")
            nt = None
            for i in range(8):
                kc = grp * 8 + i
                nt = P.op("tensor", lambda e, kc=kc, i=i, par=par, bank=bank: e.transpose(
                    bank[:, i * 128:(i + 1) * 128],
                    gout[:, par].rearrange("p a b -> p (a b)")[:, kc * 128:(kc + 1) * 128],
                    idb_sb[:, :]), "pt")
            P.wait("vector", "pt", nt)
            P.guard_wait("vector", f"wtb{par}")
            nv = P.op("vector", lambda e, grp=grp, par=par, bank=bank: e.tensor_copy(
                wtb[:, par, grp * 8:(grp + 1) * 8, :].rearrange("p a b -> p (a b)"),
                bank[:, :]), "pv")
            P.guard_set(f"gpbT{grp % 2}", "pv", nv)
        P.wait("sync", "pv", nv)
        if mode == "gu":
            nod = P.dma("sync",
                        out_dram[:, rt * 128:(rt + 1) * 128].rearrange("(a p) b -> p a b", p=128),
                        wtb[:, par], f"od{par}")
        else:
            nod = P.dma("sync",
                        out_dram[rt * 128:(rt + 1) * 128, :].rearrange("p (a b) -> p a b", b=128),
                        wtb[:, par], f"od{par}")
        P.guard_set(f"wtb{par}", f"od{par}", nod)
        return nod

    for mat, idx_sb, out_dram in (("g", ig, wgT_o), ("u", iu, wuT_o)):
        for rt in range(11):
            par = rt % 2
            P.guard_wait("gpsimd", f"gout{par}")
            ng = emit_tile(par, lambda ccall, rt=rt, idx_sb=idx_sb: idx_sb[:, rt, ccall:ccall + 1],
                           512, f"gt{par}")
            P.wait("tensor", f"gt{par}", ng)
            transpose_and_store(rt, par, out_dram, "gu")
            P.guard_set(f"gout{par}", "pt", P.cnt["pt"])
    for ic in range(11):
        par = ic % 2
        P.guard_wait("gpsimd", f"gout{par}")
        n = None
        for dt in range(32):
            for cc in range(16):
                n = P.inc(f"gt{par}", 16)
                s = P.sem[f"gt{par}"]

                def call(e, ic=ic, dt=dt, cc=cc, par=par, s=s):
                    return e.indirect_dma_start(
                        out=gout[:, par].rearrange("p a b -> p (a b)")[:, (dt * 16 + cc) * 8:(dt * 16 + cc + 1) * 8],
                        out_offset=None, in_=cb[:, :],
                        in_offset=IOA(ap=idn[:, ic, dt, cc:cc + 1], axis=0)).then_inc(s, 16)
                P.emit("gpsimd", call)
        P.wait("tensor", f"gt{par}", n)
        transpose_and_store(ic, par, wdT_o, "wd")
        P.guard_set(f"gout{par}", "pt", P.cnt["pt"])
    P.wait("sync", "od0", P.cnt["od0"])
    P.wait("sync", "od1", P.cnt["od1"])
    return P


def host_l1_inputs(c, inputs):
    cb = np.asarray(inputs["L0_ffn_codebook"], np.float32)
    ig = np.asarray(inputs["L0_ffn_idx_gate"]).astype(np.uint32)
    iu = np.asarray(inputs["L0_ffn_idx_up"]).astype(np.uint32)
    idn = np.asarray(inputs["L0_ffn_idx_down"]).astype(np.uint32)

    def pad_rows(a):
        sl = a[c * IC_PER_CORE:(c + 1) * IC_PER_CORE]
        p = np.zeros((1408, a.shape[1]), np.uint32)
        p[:1376] = sl
        return np.ascontiguousarray(p.reshape(11, 128, a.shape[1]).transpose(1, 0, 2))

    dsl = idn[:, c * 172:(c + 1) * 172]
    dp = np.zeros((4096, 176), np.uint32)
    dp[:, :172] = dsl
    d4 = dp.reshape(32, 128, 11, 16).transpose(1, 2, 0, 3)
    return {
        "cb": cb,
        "idxg": pad_rows(ig),
        "idxu": pad_rows(iu),
        "idxd": np.ascontiguousarray(d4),
        "c_ident_bf": np.eye(128, dtype=np.float32).astype(BF),
        **common_host(),
    }



def build_la_gather_qkv():
    P = Prog()
    cb = P.din("cb", [CB, 8], F32)
    idxg = P.din("idxg", [128, 11, 512], U32)
    idxu = P.din("idxu", [128, 11, 512], U32)
    idxd = P.din("idxd", [128, 11, 32, 16], U32)
    wgT_o = P.dout("wgT_o", [4096, 1408], BF16)
    wuT_o = P.dout("wuT_o", [4096, 1408], BF16)
    wdT_o = P.dout("wdT_o", [1408, 4096], BF16)
    x = P.din("x_fm", [128, NKC, 1024], F32)
    w = P.din("w_qkv", [6144, 4096], F32)
    bq = P.din("bqkv", [128, 48], F32)
    cos = P.din("cos_own", [128, 1024], F32)
    sin = P.din("sin_own", [128, 1024], F32)
    rotT = P.din("c_rot_T", [128, 128], F32)
    identb = P.din("c_ident_bf", [128, 128], BF16)
    q_out = P.dout("q_fm", [32, 128, 1024], BF16)
    k_out = P.dout("k_fm", [8, 128, 1024], BF16)
    v_out = P.dout("v_tm", [8, 1024, 128], BF16)

    GDEPTH = 3
    ixt = P.sbuf("ixt", [128, 2, 512], U32)
    gout = P.sbuf("gout", [128, GDEPTH, 512, 8], BF16)
    wtb = P.sbuf("wtb", [128, 1, 32, 128], BF16)
    idb_sb = P.sbuf("idb_sb", [128, 128], BF16)
    alloc_common(P)
    alloc_ln_scratch(P, 128)
    x_t = P.sbuf("x_t", [128, NKC, 128], F32)
    xn = P.sbuf("xn", [128, NKC, 512], BF16)
    bq_sb = P.sbuf("bq_sb", [128, 48], F32)
    cos_sb = P.sbuf("cos_sb", [128, 1024], F32)
    sin_sb = P.sbuf("sin_sb", [128, 1024], F32)
    rot_sb = P.sbuf("rot_sb", [128, 128], F32)
    wf = P.sbuf("wf", [128, 4096], F32)
    wb = P.sbuf("wb", [128, 4096], BF16)
    wt = P.sbuf("wt", [128, 2, NKC, 128], BF16)
    praw = P.sbuf("praw", [128, 1, 512], F32)
    rope1 = P.sbuf("rope1", [128, 1, 512], F32)
    obuf = P.sbuf("obuf", [128, 2, 512], BF16)
    vtm = P.sbuf("vtm", [128, 2, 4, 128], BF16)

    pb = [P.psum(f"pb{i}", [128, 512], F32) for i in range(4)]
    pbT = [P.psum(f"pbT{i}", [128, 1024], BF16) for i in range(2)]
    pbR = P.psum("pbR", [128, 512], F32)
    pbV = P.psum("pbV", [128, 512], BF16)

    P.sem_new("c0", "ld", "xd", "pt", "pv", "od", "gld0", "gld1",
              "gt0", "gt1", "gt2", "gt3", "god0", "god1", "gtr")
    n0 = load_common(P)
    P.dma("sync", bq_sb[:, :], bq, "ld")
    P.dma("sync", cos_sb[:, :], cos, "ld")
    P.dma("sync", sin_sb[:, :], sin, "ld")
    P.dma("sync", rot_sb[:, :], rotT, "ld")
    nld = P.dma("sync", idb_sb[:, :], identb, "ld")
    for e in ("scalar", "vector", "tensor"):
        P.wait(e, "ld", nld)
        P.wait(e, "c0", n0)


    tiles = [("gu", 0, rt) for rt in range(11)] +             [("gu", 1, rt) for rt in range(11)] +             [("wd", 2, ic) for ic in range(11)]
    for ti, tile in enumerate(tiles):
        par = ti % GDEPTH
        ipar = ti % 2
        gsem = f"gt{par}"
        if tile[0] == "gu":
            _, mat, rt = tile
            isrc = idxg if mat == 0 else idxu
            ngl = P.dma("gpsimd", ixt[:, ipar, :], isrc[:, rt, :], f"gld{ipar}")
        else:
            _, _, ic = tile
            ngl = P.dma("gpsimd", ixt[:, ipar, :],
                        idxd[:, ic].rearrange("p a b -> p (a b)"), f"gld{ipar}")
        P.wait("gpsimd", f"gld{ipar}", ngl)
        if ti >= GDEPTH:
            P.wait("gpsimd", "gtr", ti - GDEPTH + 1)
        n = None
        for ccall in range(512):
            n = P.inc(gsem, 16)
            s = P.sem[gsem]

            def call(e, ccall=ccall, par=par, ipar=ipar, s=s):
                return e.indirect_dma_start(
                    out=gout[:, par, ccall, :], out_offset=None, in_=cb[:, :],
                    in_offset=IOA(ap=ixt[:, ipar, ccall:ccall + 1], axis=0)).then_inc(s, 16)
            P.emit("gpsimd", call)
        tile_done = n
        tiles[ti] = tile + (tile_done,)

    for tt in range(2):
        for st_ in range(4):
            st_g = tt * 4 + st_
            P.guard_wait("sync", "x_t")
            nxd = P.dma("sync", x_t[:, :, :], x[:, :, st_g * 128:(st_g + 1) * 128], "xd")
            for e in ("scalar", "tensor", "vector"):
                P.wait(e, "xd", nxd)
            P.guard_wait("vector", "xnbuf")
            nv = phase_ln(P, x_t[:, :, :], 128, xn[:, :, st_ * 128:(st_ + 1) * 128],
                          f"ln{st_g}", pb[0][0:1, :128], pb[1][0:1, :128],
                          pb[2][:, :128], pb[3][:, :128])
            P.guard_set("x_t", f"ln{st_g}v", nv)
        P.wait("tensor", f"ln{tt * 4 + 3}v", P.cnt[f"ln{tt * 4 + 3}v"])
        for oc in range(48):
            par = oc % 2
            nw = phase_wstripe_T(P, w, oc * 128, wf, wb, wt[:, par], idb_sb, pbT, f"s{par}")
            P.wait("tensor", f"s{par}v", nw)
            g, slot = oc // 6, oc % 6
            is_q, is_k = slot < 4, slot == 4
            bank = pb[tt]
            P.guard_wait("tensor", f"pb{tt}")
            nt = None
            for kc in range(NKC):
                nt = P.op("tensor", lambda e, kc=kc, tt=tt, bank=bank, par=par: e.matmul(
                    bank[:, :], wt[:, par, kc, :], xn[:, kc, :],
                    start=(kc == 0), stop=(kc == NKC - 1)), "pt")
            P.guard_set(f"s{par}_wtv", "pt", nt)
            P.wait("vector", "pt", nt)
            P.guard_wait("vector", f"praw{tt}")
            nv = P.op("vector", lambda e, tt=tt, oc=oc, bank=bank: e.tensor_scalar(
                praw[:, 0, :], bank[:, :], bq_sb[:, oc:oc + 1], None, ALU.add), "pv")
            P.guard_set(f"pb{tt}", "pv", nv)
            if is_q or is_k:
                P.wait("tensor", "pv", nv)
                P.guard_wait("tensor", "pbR")
                ntr = P.op("tensor", lambda e, tt=tt: e.matmul(
                    pbR[:, :], rot_sb[:, :], praw[:, 0, :], start=True, stop=True), "pt")
                P.wait("vector", "pt", ntr)
                cs = cos_sb[:, tt * 512:(tt + 1) * 512]
                sn = sin_sb[:, tt * 512:(tt + 1) * 512]
                P.op("vector", lambda e, tt=tt, cs=cs: e.tensor_tensor(
                    rope1[:, 0, :], praw[:, 0, :], cs, ALU.mult))
                nv2 = P.op("vector", lambda e, tt=tt, sn=sn: e.tensor_tensor(
                    praw[:, 0, :], pbR[:, :], sn, ALU.mult), "pv")
                P.guard_set("pbR", "pv", nv2)
                P.guard_set(f"praw{tt}", "pt", ntr)
                P.guard_wait("vector", f"obuf{tt}")
                nv3 = P.op("vector", lambda e, tt=tt: e.tensor_tensor(
                    obuf[:, tt, :], rope1[:, 0, :], praw[:, 0, :], ALU.add), "pv")
                P.wait("sync", "pv", nv3)
                if is_q:
                    nod = P.dma("sync", q_out[g * 4 + slot, :, tt * 512:(tt + 1) * 512],
                                obuf[:, tt, :], "od")
                else:
                    nod = P.dma("sync", k_out[g, :, tt * 512:(tt + 1) * 512],
                                obuf[:, tt, :], "od")
                P.guard_set(f"obuf{tt}", "od", nod)
            elif slot == 5:
                P.guard_wait("vector", f"obuf{tt}")
                nv2 = P.op("vector", lambda e, tt=tt: e.tensor_copy(
                    obuf[:, tt, :], praw[:, 0, :]), "pv")
                P.wait("tensor", "pv", nv2)
                P.guard_wait("tensor", "pbV")
                ntv = None
                for i in range(4):
                    ntv = P.op("tensor", lambda e, tt=tt, i=i: e.transpose(
                        pbV[:, i * 128:(i + 1) * 128],
                        obuf[:, tt, i * 128:(i + 1) * 128], idb_sb[:, :]), "pt")
                P.guard_set(f"obuf{tt}", "pt", ntv)
                P.wait("vector", "pt", ntv)
                P.guard_wait("vector", f"vtm{tt}")
                nv3 = P.op("vector", lambda e, tt=tt: e.tensor_copy(
                    vtm[:, tt, :, :].rearrange("p a b -> p (a b)"), pbV[:, :]), "pv")
                P.guard_set("pbV", "pv", nv3)
                P.wait("sync", "pv", nv3)
                nod = P.dma("sync",
                            v_out[g, tt * 512:(tt + 1) * 512, :].rearrange("(a p) b -> p a b", p=128),
                            vtm[:, tt, :, :], "od")
                P.guard_set(f"vtm{tt}", "od", nod)
        P.guard_set("xnbuf", "pt", P.cnt["pt"])

    P.wait("tensor", "pv", P.cnt["pv"])
    P.wait("tensor", "s0v", P.cnt["s0v"])
    P.wait("tensor", "s1v", P.cnt["s1v"])
    for ti, tile in enumerate(tiles):
        par = ti % GDEPTH
        P.wait("tensor", f"gt{par}", tile[-1])
        nv = None
        for grp in range(4):
            bank = pbT[grp % 2]
            P.guard_wait("tensor", f"gpbT{grp % 2}")
            nt = None
            for i in range(8):
                kc = grp * 8 + i
                last = (grp == 3 and i == 7)
                sems = "gtr" if last else "pt"
                nt = P.op("tensor", lambda e, kc=kc, i=i, par=par, bank=bank: e.transpose(
                    bank[:, i * 128:(i + 1) * 128],
                    gout[:, par].rearrange("p a b -> p (a b)")[:, kc * 128:(kc + 1) * 128],
                    idb_sb[:, :]), sems)
                if last:
                    ntr_count = nt
                    nt = P.inc("pt", 0)
            P.wait("vector", "gtr" if grp == 3 else "pt", ntr_count if grp == 3 else nt)
            P.guard_wait("vector", "wtbA")
            nv = P.op("vector", lambda e, grp=grp, ti=ti, bank=bank: e.tensor_copy(
                wtb[:, 0, grp * 8:(grp + 1) * 8, :].rearrange("p a b -> p (a b)"),
                bank[:, :]), "pv")
            P.guard_set(f"gpbT{grp % 2}", "pv", nv)
        P.wait("scalar", "pv", nv)
        if tile[0] == "gu":
            _, mat, rt = tile[:3]
            out_dram = wgT_o if mat == 0 else wuT_o
            nod = P.dma("scalar",
                        out_dram[:, rt * 128:(rt + 1) * 128].rearrange("(a p) b -> p a b", p=128),
                        wtb[:, 0], f"god{ti % 2}")
        else:
            ic = tile[2]
            nod = P.dma("scalar",
                        wdT_o[ic * 128:(ic + 1) * 128, :].rearrange("p (a b) -> p a b", b=128),
                        wtb[:, 0], f"god{ti % 2}")
        P.guard_set("wtbA", f"god{ti % 2}", nod)
    P.wait("sync", "od", P.cnt["od"])
    P.wait("scalar", "god0", P.cnt["god0"])
    P.wait("scalar", "god1", P.cnt["god1"])
    return P


def host_la_inputs(c, inputs):
    d = host_l1_inputs(c, inputs)
    d.update(host_l2_inputs(c, inputs))
    return d



_PROGS = {}


def _get_prog(name, builder):
    if name not in _PROGS:
        P = builder()
        _PROGS[name] = (P, P.build())
    return _PROGS[name]


def _run(name, builder, in_maps):
    from concourse.bass_utils import run_bass_kernel_spmd
    P, nc = _get_prog(name, builder)
    res = run_bass_kernel_spmd(nc, in_maps, core_ids=list(range(8)))
    return [res.results[i] for i in range(8)]


def kernel(**inputs):
    ins = {k: np.asarray(v) for k, v in inputs.items()}

    x0 = np.asarray(ins["L0_attn_hidden_states"], np.float32)[0]
    x_fms = {}
    for c in range(8):
        x_fms[c] = fm(x0[own_token_ids(c)]).reshape(128, NKC, 1024)
    outs = _run("la", build_la_gather_qkv, [host_la_inputs(c, ins) for c in range(8)])
    WgT = np.concatenate([np.asarray(o["wgT_o"]).view(BF)[:, :IC_PER_CORE] for o in outs], axis=1)
    WuT = np.concatenate([np.asarray(o["wuT_o"]).view(BF)[:, :IC_PER_CORE] for o in outs], axis=1)
    WdT = np.concatenate([np.asarray(o["wdT_o"]).view(BF)[:IC_PER_CORE, :] for o in outs], axis=0)
    WgT = np.ascontiguousarray(WgT)
    WuT = np.ascontiguousarray(WuT)
    WdT = np.ascontiguousarray(WdT)
    K_full = np.zeros((8, 128, SEQ), BF)
    V_full = np.zeros((8, SEQ, 128), BF)
    q_sh = {}
    for c in range(8):
        ids = own_token_ids(c)
        q_sh[c] = np.asarray(outs[c]["q_fm"]).view(BF)
        kf = np.asarray(outs[c]["k_fm"]).view(BF)
        vf = np.asarray(outs[c]["v_tm"]).view(BF)
        for g in range(8):
            K_full[g][:, ids] = kf[g]
            V_full[g][ids, :] = vf[g]

    ims = [host_l3_inputs(c, ins, x_fms[c], q_sh[c], K_full, V_full) for c in range(8)]
    outs = _run("l3", build_l3_attn0, ims)
    x1 = {c: np.ascontiguousarray(np.asarray(outs[c]["x1_fm"], np.float32).reshape(128, NKC, 1024))
          for c in range(8)}

    ims = [host_l4_inputs(c, x1[c], WgT, WuT, WdT) for c in range(8)]
    outs = _run("l4", build_l4_ffn0, ims)
    x2 = {c: np.ascontiguousarray(np.asarray(outs[c]["x2_fm"], np.float32).reshape(128, NKC, 1024))
          for c in range(8)}

    ims = [host_l5_inputs(c, ins, x2[c]) for c in range(8)]
    outs = _run("l5", build_l5_qkv1, ims)
    K1 = np.zeros((128, SEQ), BF)
    V1 = np.zeros((SEQ, 128), BF)
    q1_sh = {}
    for c in range(8):
        ids = own_token_ids(c)
        q1_sh[c] = np.asarray(outs[c]["q1_fm"]).view(BF)
        K1[:, ids] = np.asarray(outs[c]["k1_fm"]).view(BF)
        V1[ids, :] = np.asarray(outs[c]["v1_tm"]).view(BF)

    ims = [host_l6_inputs(c, ins, x2[c], q1_sh[c], np.ascontiguousarray(K1),
                          np.ascontiguousarray(V1)) for c in range(8)]
    outs = _run("l6", build_l6_attn1, ims)
    x3 = {c: np.ascontiguousarray(np.asarray(outs[c]["x3_fm"], np.float32).reshape(128, NKC, 1024))
          for c in range(8)}

    ims = [host_l7_inputs(c, ins, x3[c]) for c in range(8)]
    outs = _run("l7", build_l7_ffn1, ims)
    out = np.zeros((SEQ, D), np.float32)
    for c in range(8):
        ids = own_token_ids(c)
        x4 = np.asarray(outs[c]["x4_fm"], np.float32).reshape(128, NKC, 1024)
        out[ids] = un_fm(x4)
    return out[None]
